# version 18
# baseline (speedup 1.0000x reference)
import math
from contextlib import ExitStack

import numpy as np
import ml_dtypes
import concourse.bass as bass
import concourse.mybir as mybir
from concourse.bass_utils import run_bass_kernel_spmd

F32 = mybir.dt.float32
BF16 = mybir.dt.bfloat16
AF = mybir.ActivationFunctionType
ALU = mybir.AluOpType
AX = mybir.AxisListType

E = 2048
NE = E // 128
H = 16
HD = 128
INTER = 8192
NI = INTER // 128
RINTER = 1024
NRI = RINTER // 128
SCALE = 1.0 / math.sqrt(HD)
EPS = 1e-5
NEG = -1.0e30

ENGINES = ("tensor", "vector", "scalar", "gpsimd", "sync")


def _splits(total, step):
    return [(o, min(step, total - o)) for o in range(0, total, step)]


class Prog:

    def __init__(self, nc, ctx):
        self.nc = nc
        self.q = {e: [] for e in ENGINES}
        self.sems = {}
        self.cnt = {}
        for name in ("st", "sv", "ss", "sge", "sgc", "sga", "sgb", "sgs",
                     "sgw0", "sgw1", "sya", "syb",
                     "sgx0", "sgx1", "sgx2", "sgx3",
                     "sxa", "sxb", "sxw", "sxo0", "sxo1", "sxd"):
            self.sems[name] = ctx.enter_context(nc.semaphore(name))
            self.cnt[name] = 0

    def do(self, eng, fn, sem=None, k=1):
        if sem is None:
            self.q[eng].append(fn)
            return None
        self.cnt[sem] += k
        m = self.cnt[sem]
        h = self.sems[sem]
        self.q[eng].append(lambda e, fn=fn, h=h, k=k: fn(e).then_inc(h, k))
        return m

    def wait(self, eng, sem, val):
        if val is None or val <= 0:
            return
        assert val <= self.cnt[sem], (sem, val, self.cnt[sem])
        h = self.sems[sem]
        self.q[eng].append(lambda e, h=h, v=val: e.wait_ge(h, v))

    def mark(self, eng, sem, k=1):
        self.cnt[sem] += k
        m = self.cnt[sem]
        h = self.sems[sem]
        prev = self.q[eng].pop()
        self.q[eng].append(lambda e, prev=prev, h=h, k=k: prev(e).then_inc(h, k))
        return m

    def dma(self, eng, out_ap, in_ap, ch):
        return self.do(eng, lambda e, o=out_ap, i=in_ap: e.dma_start(o, i),
                       ch, 16)

    def flush(self, name):
        qs = self.q
        self.q = {e: [] for e in ENGINES}
        if sum(len(v) for v in qs.values()) == 0:
            return
        with self.nc.Block(name) as blk:
            for eng in ENGINES:
                lst = qs[eng]
                if lst:
                    getattr(blk, eng)(lambda e, lst=lst: [f(e) for f in lst])


def build_program(S, dbg=False):
    nc = bass.Bass("TRN2", target_bir_lowering=False,
                   detect_race_conditions=False)
    NS = S // 128
    TSL = min(512, S)
    NSL = S // TSL
    NTT = TSL // 128

    xT = nc.dram_tensor("xT", [E, S], F32, kind="ExternalInput").ap()
    wqkv = nc.dram_tensor("wqkv", [48, 128, E], F32, kind="ExternalInput").ap()
    bqkv = nc.dram_tensor("bqkv", [128, 48], F32, kind="ExternalInput").ap()
    wout = nc.dram_tensor("wout", [16, 128, E], F32, kind="ExternalInput").ap()
    boutd = nc.dram_tensor("bout", [128, 16], F32, kind="ExternalInput").ap()
    routr = nc.dram_tensor("routr", [128, 128], F32, kind="ExternalInput").ap()
    wsg = nc.dram_tensor("wsg", [NI, 128, E], F32, kind="ExternalInput").ap()
    wsu = nc.dram_tensor("wsu", [NI, 128, E], F32, kind="ExternalInput").ap()
    wsd = nc.dram_tensor("wsd", [NE, 128, INTER], F32, kind="ExternalInput").ap()
    wrg = nc.dram_tensor("wrg", [NRI, 128, E], F32, kind="ExternalInput").ap()
    wru = nc.dram_tensor("wru", [NRI, 128, E], F32, kind="ExternalInput").ap()
    wrd = nc.dram_tensor("wrd", [NE, 128, RINTER], F32, kind="ExternalInput").ap()
    identb = nc.dram_tensor("identb", [128, 128], BF16, kind="ExternalInput").ap()
    identf = nc.dram_tensor("identf", [128, 128], F32, kind="ExternalInput").ap()
    causal = nc.dram_tensor("causal", [128, 128], F32, kind="ExternalInput").ap()
    outT = nc.dram_tensor("outT", [E, S], F32, kind="ExternalOutput").ap()
    wsgb = nc.dram_tensor("wsgb", [NI, 128, E], BF16, kind="Internal").ap()
    wsub = nc.dram_tensor("wsub", [NI, 128, E], BF16, kind="Internal").ap()
    wsdb = nc.dram_tensor("wsdb", [NE, 128, INTER], BF16, kind="Internal").ap()
    wrgb = nc.dram_tensor("wrgb", [NRI, 128, E], BF16, kind="Internal").ap()
    wrub = nc.dram_tensor("wrub", [NRI, 128, E], BF16, kind="Internal").ap()
    wrdb = nc.dram_tensor("wrdb", [NE, 128, RINTER], BF16, kind="Internal").ap()
    xres = nc.dram_tensor(
        "xres", [E, S], F32,
        kind="ExternalOutput" if dbg else "Internal").ap()

    dbg_t = {}
    if dbg:
        for nm, shp, dt in [
            ("d_ln1", [128, NE * S], BF16), ("d_q0", [128, S], BF16),
            ("d_k0", [128, S], BF16), ("d_v0", [128, S], BF16),
            ("d_p0", [128, S], BF16), ("d_av", [128, H * S], BF16),
            ("d_attn", [128, NE * S], BF16), ("d_ln2", [128, NE * TSL], BF16),
            ("d_h2", [128, 2 * TSL], BF16), ("d_ffn", [128, NE * TSL], BF16),
            ("d_comb", [128, NTT], F32), ("d_sums", [128, S // 128], F32),
            ("d_rs", [128, S // 128], F32),
        ]:
            dbg_t[nm] = nc.dram_tensor(nm, shp, dt, kind="ExternalOutput").ap()

    ctx = ExitStack()
    with ctx:
        en = ctx.enter_context
        P = Prog(nc, ctx)

        identb_sb = en(nc.sbuf_tensor("identb_sb", [128, 128], BF16))
        identf_sb = en(nc.sbuf_tensor("identf_sb", [128, 128], F32))
        causal_sb = en(nc.sbuf_tensor("causal_sb", [128, 128], F32))
        bq_sb = en(nc.sbuf_tensor("bq_sb", [128, 48], F32))
        bqs_sb = en(nc.sbuf_tensor("bqs_sb", [128, 16], F32))
        bo_sb = en(nc.sbuf_tensor("bo_sb", [128, 16], F32))
        rt_sb = en(nc.sbuf_tensor("rt_sb", [128, 128], BF16))
        ones_b = en(nc.sbuf_tensor("ones_b", [128, 1], BF16))
        ones_f = en(nc.sbuf_tensor("ones_f", [1, 128], F32))
        BW = max(NE * S, NI * TSL)
        B = en(nc.sbuf_tensor("B", [128, BW], BF16))
        bc2_r = en(nc.sbuf_tensor("bc2_r", [128, S], BF16))
        bc2_n = en(nc.sbuf_tensor("bc2_n", [128, S], BF16))

        def Bc(k):
            return B[:, k * S:(k + 1) * S]

        def ln_statmath(sum_ps, sq_ps, row_a, row_b, scr, width):
            a = row_a[0:1, 0:width]
            b = row_b[0:1, 0:width]
            s = scr[0:1, 0:width]
            P.do("vector", lambda e: e.tensor_scalar_mul(a, sum_ps, 1.0 / E), None)
            P.do("vector", lambda e: e.tensor_scalar_mul(b, sq_ps, 1.0 / E), None)
            P.do("vector", lambda e: e.tensor_tensor(s, a, a, ALU.mult), None)
            P.do("vector", lambda e: e.tensor_tensor(b, b, s, ALU.subtract), None)
            sdone = P.do("vector", lambda e: e.tensor_scalar_add(b, b, EPS), "sv")
            P.wait("scalar", "sv", sdone)
            P.do("scalar", lambda e: e.activation(b, b, AF.Ln), None)
            rdone = P.do("scalar", lambda e: e.activation(b, b, AF.Exp, scale=-0.5), "ss")
            P.wait("vector", "ss", rdone)
            ndone = P.do("vector", lambda e: e.scalar_tensor_tensor(
                a, a, -1.0, b, ALU.mult, ALU.mult), "sv")
            return rdone, ndone

        def bcast(row_sb, dst_sb, ps, width, pre_t=(), pre_v=()):
            for sem, val in pre_t:
                P.wait("tensor", sem, val)
            for (o, w) in _splits(width, 512):
                P.do("tensor", lambda e, o=o, w=w: e.matmul(
                    ps[:, o:o + w], ones_f[:, :], row_sb[0:1, o:o + w],
                    start=True, stop=True), None)
            mm = P.mark("tensor", "st")
            P.wait("vector", "st", mm)
            for sem, val in pre_v:
                P.wait("vector", sem, val)
            return P.do("vector", lambda e, w=width: e.tensor_copy(
                dst_sb[:, 0:w], ps[:, 0:w]), "sv")

        with ExitStack() as sc1:
            e1 = sc1.enter_context
            A = e1(nc.sbuf_tensor("A", [128, NE * S], BF16))
            W4 = [e1(nc.sbuf_tensor(f"W4_{s}", [128, E], BF16)) for s in range(4)]
            WCH = ("sgw0", "sgw1", "sga", "sgb")
            wstep_m = {}

            def wdma_step(n):
                h, t = divmod(n, 3)
                s = n % 4
                wstep_m[n] = P.dma("gpsimd", W4[s][:, :], wqkv[16 * t + h],
                                   WCH[s])

            def Ac(k):
                return A[:, k * S:(k + 1) * S]

            P.dma("gpsimd", identb_sb[:, :], identb, "sgc")
            P.dma("gpsimd", identf_sb[:, :], identf, "sgc")
            P.dma("gpsimd", causal_sb[:, :], causal, "sgc")
            P.dma("gpsimd", bq_sb[:, :], bqkv, "sgc")
            P.dma("gpsimd", bo_sb[:, :], boutd, "sgc")
            P.dma("gpsimd", rt_sb[:, :], routr, "sgc")
            P.do("gpsimd", lambda e: e.memset(ones_b[:, :], 1.0), "sge")
            MEMS = P.do("gpsimd", lambda e: e.memset(ones_f[:, :], 1.0), "sge")
            CONSTS = P.cnt["sgc"]
            P.wait("vector", "sgc", CONSTS)
            P.wait("vector", "sge", MEMS)
            P.do("vector", lambda e: e.tensor_scalar_mul(
                bqs_sb[:, :], bq_sb[:, 0:16], SCALE), None)

            with ExitStack() as pctx:
                pe = pctx.enter_context
                ps1 = pe(nc.psum_tensor("ps1", [128, S], F32))
                ps2 = pe(nc.psum_tensor("ps2", [128, S], F32))
                sqs = [pe(nc.sbuf_tensor(f"sqs{i}", [128, S], BF16)) for i in range(2)]
                tmp1 = pe(nc.sbuf_tensor("tmp1", [128, S], F32))
                row_a = pe(nc.sbuf_tensor("row_a", [1, S], F32))
                row_b = pe(nc.sbuf_tensor("row_b", [1, S], F32))

                xbf_m = {}
                sq_done = {}
                for k in range(min(4, NE)):
                    xbf_m[k] = P.dma("gpsimd", Bc(k),
                                     xT[k * 128:(k + 1) * 128, :], f"sgx{k % 4}")
                P.wait("tensor", "sgc", CONSTS)
                P.wait("tensor", "sge", MEMS)
                sqm = [None, None]
                last_mm = None
                for k in range(NE):
                    chan = f"sgx{k % 4}"
                    P.wait("vector", chan, xbf_m[k])
                    if sqm[k % 2] is not None:
                        P.wait("vector", "st", sqm[k % 2])
                    sq_done[k] = P.do("vector", lambda e, k=k: e.tensor_tensor(
                        sqs[k % 2][:, :], Bc(k), Bc(k), ALU.mult), "sv")
                    if k + 4 < NE:
                        P.wait("gpsimd", "sv", sq_done[k])
                        xbf_m[k + 4] = P.dma("gpsimd", Bc(k + 4),
                                             xT[(k + 4) * 128:(k + 5) * 128, :],
                                             f"sgx{(k + 4) % 4}")
                    P.wait("tensor", "sv", sq_done[k])
                    P.wait("tensor", chan, xbf_m[k])
                    for (o, w) in _splits(S, 512):
                        P.do("tensor", lambda e, o=o, w=w, k=k: e.matmul(
                            ps1[0:1, o:o + w], ones_b[:, :], Bc(k)[:, o:o + w],
                            start=(k == 0), stop=(k == NE - 1),
                            skip_group_check=True), None)
                    for (o, w) in _splits(S, 512):
                        P.do("tensor", lambda e, o=o, w=w, k=k, p=k % 2: e.matmul(
                            ps2[0:1, o:o + w], ones_b[:, :], sqs[p][:, o:o + w],
                            start=(k == 0), stop=(k == NE - 1),
                            skip_group_check=True), None)
                    last_mm = P.mark("tensor", "st")
                    sqm[k % 2] = last_mm
                P.wait("vector", "st", last_mm)
                rdone, ndone = ln_statmath(ps1[0:1, :], ps2[0:1, :],
                                           row_a, row_b, tmp1, S)
                P.wait("tensor", "ss", rdone)
                for (o, w) in _splits(S, 512):
                    P.do("tensor", lambda e, o=o, w=w: e.matmul(
                        ps1[:, o:o + w], ones_f[:, :], row_b[0:1, o:o + w],
                        start=True, stop=True), None)
                P.wait("tensor", "sv", ndone)
                for (o, w) in _splits(S, 512):
                    P.do("tensor", lambda e, o=o, w=w: e.matmul(
                        ps2[:, o:o + w], ones_f[:, :], row_a[0:1, o:o + w],
                        start=True, stop=True), None)
                bc_mm = P.mark("tensor", "st")
                P.wait("vector", "st", bc_mm)
                for k in range(NE):
                    P.do("vector", lambda e, k=k: e.tensor_tensor(
                        tmp1[:, :], Bc(k), ps1[:, :], ALU.mult), None)
                    P.do("vector", lambda e, k=k: e.tensor_tensor(
                        Ac(k), tmp1[:, :], ps2[:, :], ALU.add), None)
                for n in range(min(4, 3 * H)):
                    wdma_step(n)
                if dbg:
                    mm = P.mark("vector", "sv")
                    P.wait("sync", "sv", mm)
                    P.dma("sync", dbg_t["d_ln1"], A[:, :], "sxd")
                P.flush("init_ln1")

            with ExitStack() as actx:
                ae = actx.enter_context
                apsum = ExitStack()
                ps_sc = apsum.enter_context(nc.psum_tensor("ps_sc", [128, S], F32))
                pt_ps = [apsum.enter_context(nc.psum_tensor(f"pt{i}", [128, 1024], BF16)) for i in range(2)]
                av_ps = apsum.enter_context(nc.psum_tensor("av_ps", [128, 512], F32))
                qk_ps = apsum.enter_context(nc.psum_tensor("qk_ps", [128, 512], F32))
                Qp = [ae(nc.sbuf_tensor(f"Qp{i}", [128, S], BF16)) for i in range(2)]
                Kp = [ae(nc.sbuf_tensor(f"Kp{i}", [128, S], BF16)) for i in range(2)]
                Vb = ae(nc.sbuf_tensor("Vb", [128, S], BF16))
                VT = ae(nc.sbuf_tensor("VT", [128, S], BF16))
                Pb = [ae(nc.sbuf_tensor(f"Pb{i}", [128, S], BF16)) for i in range(6)]
                pt_sb = [ae(nc.sbuf_tensor(f"ptsb{i}", [128, 1024], BF16)) for i in range(2)]
                sums = ae(nc.sbuf_tensor("sums", [128, NS], F32))
                rs = ae(nc.sbuf_tensor("rs", [128, NS], F32))
                negmx = ae(nc.sbuf_tensor("negmx", [128, 1], F32))
                mx4 = ae(nc.sbuf_tensor("mx4", [128, 4], F32))

                NOSL = len(_splits(S, 512))
                NG = 3 * NOSL

                gmm = {}
                gevac = {}

                def qkv_subgroup(h, g):
                    t, oi = divmod(g, NOSL)
                    o, w = _splits(S, 512)[oi]
                    n = 3 * h + t
                    s = n % 4
                    par = h % 2
                    P.wait("tensor", WCH[s], wstep_m[n])
                    prev = gevac.get((h, g - 1)) or gevac.get((h - 1, NG - 1))
                    P.wait("tensor", "ss", prev)
                    for k in range(NE):
                        P.do("tensor", lambda e, s=s, o=o, w=w, k=k: e.matmul(
                            qk_ps[:, 0:w], W4[s][:, k * 128:(k + 1) * 128],
                            Ac(k)[:, o:o + w], start=(k == 0),
                            stop=(k == NE - 1), skip_group_check=True), None)
                    gmm[(h, g)] = P.mark("tensor", "st")
                    if oi == NOSL - 1 and n + 4 < 3 * H:
                        P.wait("gpsimd", "st", gmm[(h, g)])
                        wdma_step(n + 4)
                    P.wait("scalar", "st", gmm[(h, g)])
                    if t == 0:
                        dst, bias_ap, sc = Qp[par], bqs_sb[:, h:h + 1], SCALE
                    elif t == 1:
                        dst, bias_ap, sc = Kp[par], bq_sb[:, 16 + h:17 + h], 1.0
                    else:
                        dst, bias_ap, sc = Vb, bq_sb[:, 32 + h:33 + h], 1.0
                    gevac[(h, g)] = P.do(
                        "scalar", lambda e, dst=dst, o=o, w=w, b=bias_ap, sc=sc: e.activation(
                            dst[:, o:o + w], qk_ps[:, 0:w], AF.Identity,
                            bias=b, scale=sc), "ss")

                stage_jobs = []
                for i in range(0, NI, 8):
                    stage_jobs.append((wsgb[i:i + 8], wsg[i:i + 8]))
                for i in range(0, NI, 8):
                    stage_jobs.append((wsub[i:i + 8], wsu[i:i + 8]))
                for i in range(0, NE, 2):
                    stage_jobs.append((wsdb[i:i + 2], wsd[i:i + 2]))
                stage_jobs.append((wrgb[:], wrg[:]))
                stage_jobs.append((wrub[:], wru[:]))
                stage_jobs.append((wrdb[:], wrd[:]))
                nstg = (len(stage_jobs) + H - 1) // H

                head_exp = None
                for h in range(H):
                    par = h % 2
                    for dst, srcap in stage_jobs[h * nstg:(h + 1) * nstg]:
                        P.dma("gpsimd", dst, srcap, "sgs")
                    if h == 0:
                        for g in range(NG):
                            qkv_subgroup(0, g)
                    slot = 0
                    P.wait("tensor", "ss", gevac[(h, NG - 1)])
                    vt_cast = {}
                    n_g4 = (NS + 3) // 4
                    for g4 in range(n_g4):
                        kts = list(range(4 * g4, min(4 * g4 + 4, NS)))
                        if g4 >= 2:
                            P.wait("tensor", "ss", vt_cast[g4 - 2])
                        for j, kt in enumerate(kts):
                            P.do("tensor", lambda e, j=j, kt=kt, g4=g4: e.transpose(
                                pt_ps[g4 % 2][:, j * 128:(j + 1) * 128],
                                Vb[:, kt * 128:(kt + 1) * 128], identb_sb[:, :]), None)
                        tg = P.mark("tensor", "st")
                        P.wait("scalar", "st", tg)
                        w = 128 * len(kts)
                        vt_cast[g4] = P.do("scalar", lambda e, g4=g4, w=w, o=4 * g4 * 128: e.copy(
                            VT[:, o:o + w], pt_ps[g4 % 2][:, 0:w]), "ss")
                    vt_done = P.cnt["ss"]

                    CH = {"prev_exp": vt_done, "slot": 0}
                    av_evac = None
                    emitted = set()
                    pscale_m = {}

                    def emit_chain(qg, Lb):
                        if qg in emitted:
                            return
                        emitted.add(qg)
                        ps = Pb[qg % 6]
                        Lq = (qg + 1) * 128
                        P.wait("tensor", "ss", CH["prev_exp"])
                        sls = _splits(Lq, 512)
                        split = Lq > 1024
                        lm = None
                        for i, (o, w) in enumerate(sls):
                            P.do("tensor", lambda e, o=o, w=w, qg=qg, par=par: e.matmul(
                                ps_sc[:, o:o + w],
                                Qp[par][:, qg * 128:(qg + 1) * 128],
                                Kp[par][:, o:o + w], start=True, stop=True,
                                skip_group_check=True), None)
                            smm = P.mark("tensor", "st")
                            if not split and i < len(sls) - 1:
                                continue
                            P.wait("vector", "st", smm)
                            d0 = qg * 128
                            if o <= d0 < o + w:
                                P.do("vector", lambda e, d0=d0: e.tensor_tensor(
                                    ps_sc[:, d0:d0 + 128],
                                    ps_sc[:, d0:d0 + 128],
                                    causal_sb[:, :], ALU.add), None)
                            if split:
                                lm = P.do("vector", lambda e, i=i, o=o, w=w: e.reduce_max(
                                    mx4[:, i:i + 1], ps_sc[:, o:o + w],
                                    axis=AX.X), "sv")
                        if split:
                            P.wait("vector", "sv", lm)
                            mx = P.do("vector", lambda e, ns=len(sls): e.reduce_max(
                                negmx[:, :], mx4[:, 0:ns], axis=AX.X,
                                negate=True), "sv")
                        else:
                            mx = P.do("vector", lambda e, Lq=Lq: e.reduce_max(
                                negmx[:, :], ps_sc[:, 0:Lq], axis=AX.X,
                                negate=True), "sv")
                        if h + 1 < H and CH["slot"] < NG and qg >= NS - NG:
                            qkv_subgroup(h + 1, CH["slot"])
                            CH["slot"] += 1
                        P.wait("scalar", "sv", mx)
                        CH["prev_exp"] = P.do("scalar", lambda e, ps=ps, Lq=Lq, qg=qg: e.activation(
                            ps[:, 0:Lq], ps_sc[:, 0:Lq], AF.Exp,
                            bias=negmx[:, :], scale=1.0,
                            accum_out=sums[:, qg:qg + 1]), "ss")
                        P.wait("vector", "ss", CH["prev_exp"])
                        rm = P.do("vector", lambda e, qg=qg: e.reciprocal(
                            rs[:, qg:qg + 1], sums[:, qg:qg + 1]), "sv")
                        P.wait("vector", "sv", rm)
                        P.do("vector", lambda e, ps=ps, Lq=Lq, qg=qg: e.tensor_scalar_mul(
                            ps[:, 0:Lq], ps[:, 0:Lq], rs[:, qg:qg + 1]), None)
                        if Lq < Lb:
                            P.do("vector", lambda e, ps=ps, Lq=Lq, Lb=Lb: e.memset(
                                ps[:, Lq:Lb], 0.0), None)
                        pscale_m[qg] = P.mark("vector", "sv")

                    for qb0 in range(0, NS, 4):
                        nqt = min(4, NS - qb0)
                        Lb = (qb0 + nqt) * 128
                        for qt in range(nqt):
                            emit_chain(qb0 + qt, Lb)
                        nxt = [qb0 + 4 + i for i in range(2)
                               if qb0 + 4 + i < NS]
                        cast_m = {}
                        av_mm = {}
                        nkt = Lb // 128
                        nbt = (nkt + 1) // 2
                        wqt = 128 * nqt
                        for bt in range(nbt):
                            kts = [k for k in (2 * bt, 2 * bt + 1) if k < nkt]
                            pp = bt % 2
                            P.wait("tensor", "sv", pscale_m[qb0 + nqt - 1])
                            if bt >= 2:
                                P.wait("tensor", cast_m[bt - 2][0], cast_m[bt - 2][1])
                            if bt == 0 and av_evac is not None:
                                P.wait("tensor", "ss", av_evac)
                            for j, kt in enumerate(kts):
                                for qt in range(nqt):
                                    P.do("tensor", lambda e, sl6=(qb0 + qt) % 6, qt=qt, kt=kt, pp=pp, j=j: e.transpose(
                                        pt_ps[pp][:, j * wqt + qt * 128:j * wqt + (qt + 1) * 128],
                                        Pb[sl6][:, kt * 128:(kt + 1) * 128],
                                        identb_sb[:, :]), None)
                            tg = P.mark("tensor", "st")

                            def nxt_lb(qg):
                                qb1 = (qg // 4) * 4
                                return (qb1 + min(4, NS - qb1)) * 128

                            if bt == 1 and nxt:
                                emit_chain(nxt[0], nxt_lb(nxt[0]))
                            elif bt == 3 and len(nxt) > 1:
                                emit_chain(nxt[1], nxt_lb(nxt[1]))
                            ceng, csem = (("vector", "sv") if bt % 2 else ("scalar", "ss"))
                            P.wait(ceng, "st", tg)
                            if bt >= 1:
                                P.wait(ceng, "st", av_mm[bt - 1])
                            cast_m[bt] = (csem, P.do(ceng, lambda e, pp=pp, w=len(kts) * wqt: (
                                e.copy if hasattr(e, "copy") else e.tensor_copy)(
                                pt_sb[pp][:, 0:w], pt_ps[pp][:, 0:w]), csem))
                            P.wait("tensor", cast_m[bt][0], cast_m[bt][1])
                            for j, kt in enumerate(kts):
                                mm = P.do("tensor", lambda e, pp=pp, kt=kt, j=j, w=wqt, nkt=nkt: e.matmul(
                                    av_ps[:, 0:w], VT[:, kt * 128:(kt + 1) * 128],
                                    pt_sb[pp][:, j * w:(j + 1) * w], start=(kt == 0),
                                    stop=(kt == nkt - 1), skip_group_check=True),
                                    "st" if kt == kts[-1] else None)
                            av_mm[bt] = mm
                        P.wait("scalar", "st", av_mm[nbt - 1])
                        av_evac = P.do("scalar", lambda e, h=h, qb0=qb0, w=128 * nqt: e.copy(
                            Bc(h)[:, qb0 * 128:qb0 * 128 + w], av_ps[:, 0:w]), "ss")
                    prev_exp = CH["prev_exp"]
                    slot = CH["slot"]
                    if h + 1 < H:
                        while slot < NG:
                            qkv_subgroup(h + 1, slot)
                            slot += 1
                    head_exp = prev_exp
                if dbg:
                    P.wait("sync", "ss", P.cnt["ss"])
                    P.wait("sync", "sv", P.cnt["sv"])
                    P.dma("sync", dbg_t["d_sums"], sums[:, :], "sxd")
                    P.dma("sync", dbg_t["d_rs"], rs[:, :], "sxd")
                    P.dma("sync", dbg_t["d_q0"], Qp[(H - 1) % 2][:, :], "sxd")
                    P.dma("sync", dbg_t["d_k0"], Kp[(H - 1) % 2][:, :], "sxd")
                    P.dma("sync", dbg_t["d_v0"], Vb[:, :], "sxd")
                    P.dma("sync", dbg_t["d_p0"], Pb[0][:, :], "sxd")
                    P.dma("sync", dbg_t["d_av"], B[:, 0:H * S], "sxd")
                P.flush("attn")
                apsum.close()

                with ExitStack() as octx:
                    oe = octx.enter_context
                    wo_ps = [oe(nc.psum_tensor(f"wo{i}", [128, S], F32)) for i in range(2)]
                    wo_dma = {0: P.dma("gpsimd", W4[0][:, :], wout[0], "sga"),
                              1: P.dma("gpsimd", W4[1][:, :], wout[1], "sgb")}
                    evj = {}
                    for j in range(NE):
                        p = j % 2
                        P.wait("tensor", "sga" if p == 0 else "sgb", wo_dma[j])
                        if j >= 2:
                            P.wait("tensor", "ss", evj[j - 2])
                        for (o, w) in _splits(S, 512):
                            for k in range(NE):
                                P.do("tensor", lambda e, o=o, w=w, k=k, p=p: e.matmul(
                                    wo_ps[p][:, o:o + w],
                                    W4[p][:, k * 128:(k + 1) * 128],
                                    Bc(k)[:, o:o + w],
                                    start=(k == 0), stop=(k == NE - 1),
                                    skip_group_check=True), None)
                        g = P.mark("tensor", "st")
                        if j + 2 < NE:
                            P.wait("gpsimd", "st", g)
                            wo_dma[j + 2] = P.dma("gpsimd", W4[p][:, :],
                                                  wout[j + 2],
                                                  "sga" if p == 0 else "sgb")
                        P.wait("scalar", "st", g)
                        evj[j] = P.do("scalar", lambda e, j=j, p=p: e.activation(
                            Ac(j)[:, :], wo_ps[p][:, :], AF.Identity,
                            bias=bo_sb[:, j:j + 1], scale=1.0), "ss")
                    if dbg:
                        P.wait("sync", "ss", P.cnt["ss"])
                        P.dma("sync", dbg_t["d_attn"], A[:, :], "sxd")
                    P.flush("wout")

            with ExitStack() as pctx:
                pe = pctx.enter_context
                ps1 = pe(nc.psum_tensor("p35_1", [128, S], F32))
                ps2 = pe(nc.psum_tensor("p35_2", [128, S], F32))
                sqs = [pe(nc.sbuf_tensor(f"p35sq{i}", [128, S], BF16)) for i in range(2)]
                xf = [pe(nc.sbuf_tensor(f"p35x{i}", [128, S], F32)) for i in range(2)]
                xr = pe(nc.sbuf_tensor("p35r", [128, S], F32))
                xrb = pe(nc.sbuf_tensor("p35rb", [128, S], BF16))
                row_a = pe(nc.sbuf_tensor("row_a2", [1, S], F32))
                row_b = pe(nc.sbuf_tensor("row_b2", [1, S], F32))

                sqm = [None, None]
                last_mm = None
                for k in range(NE):
                    if sqm[k % 2] is not None:
                        P.wait("vector", "st", sqm[k % 2])
                    sq_done = P.do("vector", lambda e, k=k: e.tensor_tensor(
                        sqs[k % 2][:, :], Ac(k), Ac(k), ALU.mult), "sv")
                    P.wait("tensor", "sv", sq_done)
                    for (o, w) in _splits(S, 512):
                        P.do("tensor", lambda e, o=o, w=w, k=k: e.matmul(
                            ps1[0:1, o:o + w], ones_b[:, :], Ac(k)[:, o:o + w],
                            start=(k == 0), stop=(k == NE - 1),
                            skip_group_check=True), None)
                    for (o, w) in _splits(S, 512):
                        P.do("tensor", lambda e, o=o, w=w, k=k, p=k % 2: e.matmul(
                            ps2[0:1, o:o + w], ones_b[:, :], sqs[p][:, o:o + w],
                            start=(k == 0), stop=(k == NE - 1),
                            skip_group_check=True), None)
                    last_mm = P.mark("tensor", "st")
                    sqm[k % 2] = last_mm
                P.wait("vector", "st", last_mm)
                rdone, ndone = ln_statmath(ps1[0:1, :], ps2[0:1, :],
                                           row_a, row_b, xr, S)
                bcr_m = bcast(row_b, bc2_r, ps1, S, pre_t=(("ss", rdone),))
                bcn_m = bcast(row_a, bc2_n, ps2, S, pre_t=(("sv", ndone),))

                xld = {0: P.dma("sync", xf[0][:, :], xT[0:128, :], "sxa"),
                       1: P.dma("sync", xf[1][:, :], xT[128:256, :], "sxb")}
                wr_m = {}
                cast_m = {}
                stat_m = {}
                sq_v = {}
                P.wait("vector", "sv", bcn_m)
                P.wait("vector", "sv", bcr_m)
                for k in range(NE):
                    p = k % 2
                    ch = "sxa" if p == 0 else "sxb"
                    P.wait("vector", ch, xld[k])
                    if k >= 1:
                        P.wait("vector", "sxw", wr_m[k - 1])
                        P.wait("vector", "ss", cast_m[k - 1])
                    P.do("vector", lambda e, k=k: e.tensor_tensor(
                        xr[:, :], Ac(k), bc2_r[:, :], ALU.mult), None)
                    P.do("vector", lambda e: e.tensor_tensor(
                        xr[:, :], xr[:, :], bc2_n[:, :], ALU.add), None)
                    vm = P.do("vector", lambda e, p=p: e.tensor_tensor(
                        xr[:, :], xr[:, :], xf[p][:, :], ALU.add), "sv")
                    if k + 2 < NE:
                        P.wait("sync", "sv", vm)
                        xld[k + 2] = P.dma("sync", xf[p][:, :],
                                           xT[(k + 2) * 128:(k + 3) * 128, :], ch)
                    P.wait("sync", "sv", vm)
                    wr_m[k] = P.dma("sync", xres[k * 128:(k + 1) * 128, :],
                                    xr[:, :], "sxw")
                    P.wait("scalar", "sv", vm)
                    if k >= 1:
                        P.wait("scalar", "sv", sq_v[k - 1])
                        P.wait("scalar", "st", stat_m[k - 1])
                    cast_m[k] = P.do("scalar", lambda e: e.copy(xrb[:, :], xr[:, :]), "ss")
                    P.wait("vector", "ss", cast_m[k])
                    if k >= 2:
                        P.wait("vector", "st", stat_m[k - 2])
                    sq_v[k] = P.do("vector", lambda e, p=p: e.tensor_tensor(
                        sqs[p][:, :], xrb[:, :], xrb[:, :], ALU.mult), "sv")
                    P.wait("tensor", "ss", cast_m[k])
                    P.wait("tensor", "sv", sq_v[k])
                    for (o, w) in _splits(S, 512):
                        P.do("tensor", lambda e, o=o, w=w, k=k: e.matmul(
                            ps1[0:1, o:o + w], ones_b[:, :], xrb[:, o:o + w],
                            start=(k == 0), stop=(k == NE - 1),
                            skip_group_check=True), None)
                    for (o, w) in _splits(S, 512):
                        P.do("tensor", lambda e, o=o, w=w, k=k, p=p: e.matmul(
                            ps2[0:1, o:o + w], ones_b[:, :], sqs[p][:, o:o + w],
                            start=(k == 0), stop=(k == NE - 1),
                            skip_group_check=True), None)
                    stat_m[k] = P.mark("tensor", "st")
                P.wait("vector", "st", stat_m[NE - 1])
                P.wait("vector", "sxw", wr_m[NE - 1])
                P.wait("vector", "ss", cast_m[NE - 1])
                rdone, ndone = ln_statmath(ps1[0:1, :], ps2[0:1, :],
                                           row_a, row_b, xr, S)
                bcast(row_b, bc2_r, ps1, S, pre_t=(("ss", rdone),))
                bcast(row_a, bc2_n, ps2, S, pre_t=(("sv", ndone),))
                P.flush("p35")

        with ExitStack() as fctx:
            fe = fctx.enter_context
            gps = [fe(nc.psum_tensor(f"gps{i}", [128, 512], F32)) for i in range(2)]
            ups = [fe(nc.psum_tensor(f"ups{i}", [128, 512], F32)) for i in range(2)]
            st_s = fe(nc.psum_tensor("st_s", [128, 512], F32))
            st_q = fe(nc.psum_tensor("st_q", [128, 512], F32))
            misc = fe(nc.psum_tensor("misc", [128, 512], F32))
            ln2_sl = fe(nc.sbuf_tensor("ln2_sl", [128, NE * TSL], BF16))
            Hr = fe(nc.sbuf_tensor("Hr", [128, NRI * TSL], BF16))
            ffn_sl = fe(nc.sbuf_tensor("ffn_sl", [128, NE * TSL], BF16))
            wg_b = [fe(nc.sbuf_tensor(f"wgb{i}", [128, E], BF16)) for i in range(2)]
            wu_b = [fe(nc.sbuf_tensor(f"wub{i}", [128, E], BF16)) for i in range(2)]
            wd_b = [fe(nc.sbuf_tensor(f"wdb{i}", [128, INTER], BF16)) for i in range(2)]
            wrd_b = [fe(nc.sbuf_tensor(f"wrdb{i}", [128, RINTER], BF16)) for i in range(2)]
            silu_s = [fe(nc.sbuf_tensor(f"silu{i}", [128, TSL], F32)) for i in range(2)]
            xp_b = [fe(nc.sbuf_tensor(f"xpb{i}", [128, TSL], F32)) for i in range(2)]
            fsq = [fe(nc.sbuf_tensor(f"fsq{i}", [128, TSL], BF16)) for i in range(2)]
            out_b = [fe(nc.sbuf_tensor(f"outb{i}", [128, TSL], F32)) for i in range(2)]
            lg_sb = fe(nc.sbuf_tensor("lg_sb", [128, 8 * NTT], F32))
            t8 = fe(nc.sbuf_tensor("t8", [128, 8], F32))
            eqm = fe(nc.sbuf_tensor("eqm", [128, 8], F32))
            v1c = fe(nc.sbuf_tensor("v1c", [128, 1], F32))
            dcol = fe(nc.sbuf_tensor("dcol", [128, 1], F32))
            ecol = fe(nc.sbuf_tensor("ecol", [128, 1], F32))
            dencol = fe(nc.sbuf_tensor("dencol", [128, 1], F32))
            comb_sb = fe(nc.sbuf_tensor("comb_sb", [128, NTT], F32))
            bc_cmb = fe(nc.sbuf_tensor("bc_cmb", [128, TSL], BF16))
            bc4r = fe(nc.sbuf_tensor("bc4r", [128, TSL], F32))
            bc4n = fe(nc.sbuf_tensor("bc4n", [128, TSL], F32))
            r4a = fe(nc.sbuf_tensor("r4a", [1, TSL], F32))
            r4b = fe(nc.sbuf_tensor("r4b", [1, TSL], F32))
            scr4 = fe(nc.sbuf_tensor("scr4", [1, TSL], F32))
            comb_row = fe(nc.sbuf_tensor("comb_row", [1, TSL], F32))

            def Hc(m):
                return B[:, m * TSL:(m + 1) * TSL]

            def L2c(k):
                return ln2_sl[:, k * TSL:(k + 1) * TSL]

            def Hrc(m):
                return Hr[:, m * TSL:(m + 1) * TSL]

            def Fc(j):
                return ffn_sl[:, j * TSL:(j + 1) * TSL]

            def pch(p):
                return "sga" if p == 0 else "sgb"

            def syp(p):
                return "sya" if p == 0 else "syb"

            P.wait("sync", "sgs", P.cnt["sgs"])
            P.dma("sync", wg_b[0][:, :], wsgb[0], "sya")
            gu0 = P.dma("sync", wu_b[0][:, :], wsub[0], "sya")
            P.dma("sync", wg_b[1][:, :], wsgb[1], "syb")
            gu1 = P.dma("sync", wu_b[1][:, :], wsub[1], "syb")
            gu_m = {0: gu0, 1: gu1}

            ln2_used = None
            H_used = None
            down_used = {0: None, 1: None}
            out_free = {0: None, 1: None}
            misc_free = None
            xp_prev = {0: None, 1: None}
            hr_prev = {0: None, 1: None}
            om_prev = {0: None, 1: None}
            SL = {}

            def emit_A(s):
                c0 = s * TSL
                xl = {}
                for k in range(2):
                    P.wait("sync", "sv", xp_prev[k])
                    xl[k] = P.dma("sync", xp_b[k][:, :],
                                  xres[k * 128:(k + 1) * 128, c0:c0 + TSL],
                                  "sxa" if k == 0 else "sxb")
                ln2_v = {}
                for k in range(NE):
                    p = k % 2
                    ch = "sxa" if p == 0 else "sxb"
                    P.wait("vector", ch, xl[k])
                    if ln2_used is not None and k == 0:
                        P.wait("vector", "st", ln2_used)
                        if dbg:
                            P.wait("vector", "sxd", P.cnt["sxd"])
                    P.do("vector", lambda e, p=p, c0=c0: e.tensor_tensor(
                        xp_b[p][:, :], xp_b[p][:, :], bc2_r[:, c0:c0 + TSL],
                        ALU.mult), None)
                    ln2_v[k] = P.do("vector", lambda e, k=k, p=p, c0=c0: e.tensor_tensor(
                        L2c(k), xp_b[p][:, :], bc2_n[:, c0:c0 + TSL], ALU.add), "sv")
                    if k + 2 < NE:
                        P.wait("sync", "sv", ln2_v[k])
                        xl[k + 2] = P.dma("sync", xp_b[p][:, :],
                                          xres[(k + 2) * 128:(k + 3) * 128, c0:c0 + TSL],
                                          ch)
                xp_prev[0] = ln2_v[NE - 2]
                xp_prev[1] = ln2_v[NE - 1]
                SL[s] = {"ln2_done": ln2_v[NE - 1]}

            def emit_B(s):
                nonlocal_misc = misc_free
                ln2_done = SL[s]["ln2_done"]
                lg_read = nonlocal_misc
                for tt in range(NTT):
                    P.wait("tensor", "sv", ln2_done)
                    P.wait("tensor", "sv", lg_read)
                    for k in range(NE):
                        P.do("tensor", lambda e, tt=tt, k=k: e.matmul(
                            misc[:, 0:8], L2c(k)[:, tt * 128:(tt + 1) * 128],
                            rt_sb[:, k * 8:(k + 1) * 8],
                            start=(k == 0), stop=(k == NE - 1),
                            skip_group_check=True), None)
                    lgm = P.mark("tensor", "st")
                    P.wait("vector", "st", lgm)
                    lg_read = P.do("vector", lambda e, tt=tt: e.tensor_copy(
                        lg_sb[:, tt * 8:(tt + 1) * 8], misc[:, 0:8]), "sv")
                comb_m = None
                for tt in range(NTT):
                    sl = lg_sb[:, tt * 8:(tt + 1) * 8]

                    def step(eng, fn):
                        sem = "sv" if eng == "vector" else "ss"
                        m = P.do(eng, fn, sem)
                        P.wait(eng, sem, m)
                        return m

                    step("vector", lambda e, sl=sl: e.reduce_max(
                        v1c[:, :], sl, axis=AX.X))
                    step("vector", lambda e, sl=sl: e.tensor_scalar(
                        t8[:, :], sl, v1c[:, :], None, ALU.subtract))
                    step("vector", lambda e: e.tensor_scalar(
                        eqm[:, :], t8[:, :], 0.0, NEG, ALU.is_equal, ALU.mult))
                    step("vector", lambda e: e.tensor_tensor(
                        t8[:, :], t8[:, :], eqm[:, :], ALU.add))
                    dm = step("vector", lambda e: e.reduce_max(
                        dcol[:, :], t8[:, :], axis=AX.X))
                    P.wait("scalar", "sv", dm)
                    em = step("scalar", lambda e: e.activation(
                        ecol[:, :], dcol[:, :], AF.Exp))
                    P.wait("vector", "ss", em)
                    step("vector", lambda e: e.tensor_scalar_add(
                        dencol[:, :], ecol[:, :], 1.0))
                    step("vector", lambda e: e.reciprocal(ecol[:, :], dencol[:, :]))
                    comb_m = step("vector", lambda e, tt=tt: e.tensor_tensor(
                        comb_sb[:, tt:tt + 1], dencol[:, :], ecol[:, :], ALU.mult))
                P.wait("tensor", "sv", comb_m)
                P.wait("tensor", "sv", lg_read)
                for tt in range(NTT):
                    P.do("tensor", lambda e, tt=tt: e.transpose(
                        misc[0:1, tt * 128:(tt + 1) * 128],
                        comb_sb[:, tt:tt + 1], identf_sb[:, :]), None)
                ctm = P.mark("tensor", "st")
                P.wait("vector", "st", ctm)
                crm = P.do("vector", lambda e: e.tensor_copy(
                    comb_row[:, :], misc[0:1, 0:TSL]), "sv")
                cb = bcast(comb_row, bc_cmb, misc, TSL, pre_t=(("sv", crm),))
                SL[s]["cb_done"] = cb
                return cb

            def emit_C(s):
                h_m = {}
                for m in range(NI):
                    p = m % 2
                    P.wait("tensor", syp(p), gu_m[m])
                    if m >= 2:
                        P.wait("tensor", "sv", h_m[m - 2])
                    for k in range(NE):
                        P.do("tensor", lambda e, k=k, p=p: e.matmul(
                            gps[p][:, 0:TSL], wg_b[p][:, k * 128:(k + 1) * 128],
                            L2c(k), start=(k == 0), stop=(k == NE - 1),
                            skip_group_check=True), None)
                    for k in range(NE):
                        P.do("tensor", lambda e, k=k, p=p: e.matmul(
                            ups[p][:, 0:TSL], wu_b[p][:, k * 128:(k + 1) * 128],
                            L2c(k), start=(k == 0), stop=(k == NE - 1),
                            skip_group_check=True), None)
                    gum = P.mark("tensor", "st")
                    if m + 2 < NI:
                        P.wait("sync", "st", gum)
                        P.dma("sync", wg_b[p][:, :], wsgb[m + 2], syp(p))
                        gu_m[m + 2] = P.dma("sync", wu_b[p][:, :], wsub[m + 2], syp(p))
                    P.wait("scalar", "st", gum)
                    if m >= 2:
                        P.wait("scalar", "sv", h_m[m - 2])
                    else:
                        P.wait("scalar", "sv", om_prev[p])
                        P.wait("scalar", "sv", hr_prev[p])
                    sm = P.do("scalar", lambda e, p=p: e.activation(
                        silu_s[p][:, :], gps[p][:, 0:TSL], AF.Silu), "ss")
                    P.wait("vector", "ss", sm)
                    if H_used is not None and m == 0:
                        P.wait("vector", "st", H_used)
                        if dbg:
                            P.wait("vector", "sxd", P.cnt["sxd"])
                    h_m[m] = P.do("vector", lambda e, m=m, p=p: e.tensor_tensor(
                        Hc(m), silu_s[p][:, :], ups[p][:, 0:TSL], ALU.mult), "sv")
                SL[s]["h_m"] = h_m

            def emit_D(s):
                h_m = SL[s]["h_m"]
                rgu = {}
                P.wait("sync", "sv", h_m[NI - 2])
                P.dma("sync", wg_b[0][:, :], wrgb[0], "sya")
                rgu[0] = P.dma("sync", wu_b[0][:, :], wrub[0], "sya")
                P.wait("sync", "sv", h_m[NI - 1])
                P.dma("sync", wg_b[1][:, :], wrgb[1], "syb")
                rgu[1] = P.dma("sync", wu_b[1][:, :], wrub[1], "syb")
                hr_m = {}
                gum = None
                for m in range(NRI):
                    p = m % 2
                    P.wait("tensor", syp(p), rgu[m])
                    if m >= 2:
                        P.wait("tensor", "sv", hr_m[m - 2])
                    for k in range(NE):
                        P.do("tensor", lambda e, k=k, p=p: e.matmul(
                            gps[p][:, 0:TSL], wg_b[p][:, k * 128:(k + 1) * 128],
                            L2c(k), start=(k == 0), stop=(k == NE - 1),
                            skip_group_check=True), None)
                    for k in range(NE):
                        P.do("tensor", lambda e, k=k, p=p: e.matmul(
                            ups[p][:, 0:TSL], wu_b[p][:, k * 128:(k + 1) * 128],
                            L2c(k), start=(k == 0), stop=(k == NE - 1),
                            skip_group_check=True), None)
                    gum = P.mark("tensor", "st")
                    if m + 2 < NRI:
                        P.wait("sync", "st", gum)
                        P.dma("sync", wg_b[p][:, :], wrgb[m + 2], syp(p))
                        rgu[m + 2] = P.dma("sync", wu_b[p][:, :], wrub[m + 2], syp(p))
                    elif s + 1 < NSL and m + 2 < NRI + 2:
                        P.wait("sync", "st", gum)
                        P.dma("sync", wg_b[p][:, :], wsgb[m + 2 - NRI], syp(p))
                        gu_m[m + 2 - NRI] = P.dma("sync", wu_b[p][:, :],
                                                  wsub[m + 2 - NRI], syp(p))
                    P.wait("scalar", "st", gum)
                    if m >= 2:
                        P.wait("scalar", "sv", hr_m[m - 2])
                    else:
                        P.wait("scalar", "sv", h_m[NI - 2 + m])
                    sm = P.do("scalar", lambda e, p=p: e.activation(
                        silu_s[p][:, :], gps[p][:, 0:TSL], AF.Silu), "ss")
                    P.wait("vector", "ss", sm)
                    hr_m[m] = P.do("vector", lambda e, m=m, p=p: e.tensor_tensor(
                        Hrc(m), silu_s[p][:, :], ups[p][:, 0:TSL], ALU.mult), "sv")
                SL[s]["hr_m"] = hr_m
                hr_prev[0] = hr_m[NRI - 2]
                hr_prev[1] = hr_m[NRI - 1]
                return gum

            def emit_E(s):
                h_m = SL[s]["h_m"]
                hr_m = SL[s]["hr_m"]
                cb_done = SL[s]["cb_done"]
                dd = {}
                P.wait("gpsimd", "st", down_used[0])
                P.dma("gpsimd", wd_b[0][:, :], wsdb[0], "sga")
                dd[0] = P.dma("gpsimd", wrd_b[0][:, :], wrdb[0], "sga")
                P.wait("gpsimd", "st", down_used[1])
                P.dma("gpsimd", wd_b[1][:, :], wsdb[1], "sgb")
                dd[1] = P.dma("gpsimd", wrd_b[1][:, :], wrdb[1], "sgb")
                h_all = h_m[NI - 1]
                hr_all = hr_m[NRI - 1]
                comb_v = {}
                stat4 = {}
                sq4 = {}
                dmm = {}
                for j in range(NE):
                    p = j % 2
                    P.wait("tensor", pch(p), dd[j])
                    P.wait("tensor", "sv", h_all)
                    P.wait("tensor", "sv", hr_all)
                    if j >= 2:
                        P.wait("tensor", "sv", comb_v[j - 2])
                    for m in range(NI):
                        P.do("tensor", lambda e, m=m, p=p: e.matmul(
                            gps[p][:, 0:TSL], wd_b[p][:, m * 128:(m + 1) * 128],
                            Hc(m), start=(m == 0), stop=(m == NI - 1),
                            skip_group_check=True), None)
                    for m in range(NRI):
                        P.do("tensor", lambda e, m=m, p=p: e.matmul(
                            ups[p][:, 0:TSL], wrd_b[p][:, m * 128:(m + 1) * 128],
                            Hrc(m), start=(m == 0), stop=(m == NRI - 1),
                            skip_group_check=True), None)
                    dmm[j] = P.mark("tensor", "st")
                    if j + 2 < NE:
                        P.wait("gpsimd", "st", dmm[j])
                        P.dma("gpsimd", wd_b[p][:, :], wsdb[j + 2], pch(p))
                        dd[j + 2] = P.dma("gpsimd", wrd_b[p][:, :], wrdb[j + 2], pch(p))
                    P.wait("vector", "st", dmm[j])
                    if j == 0:
                        P.wait("vector", "sv", cb_done)
                    P.do("vector", lambda e, j=j, p=p: e.tensor_tensor(
                        Fc(j), ups[p][:, 0:TSL], bc_cmb[:, :], ALU.mult), None)
                    comb_v[j] = P.do("vector", lambda e, j=j, p=p: e.tensor_tensor(
                        Fc(j), Fc(j), gps[p][:, 0:TSL], ALU.add), "sv")
                    if j >= 2:
                        P.wait("vector", "st", stat4[j - 2])
                    sq4[j] = P.do("vector", lambda e, j=j, p=p: e.tensor_tensor(
                        fsq[p][:, :], Fc(j), Fc(j), ALU.mult), "sv")
                    P.wait("tensor", "sv", sq4[j])
                    P.do("tensor", lambda e, j=j: e.matmul(
                        st_s[0:1, 0:TSL], ones_b[:, :], Fc(j),
                        start=(j == 0), stop=(j == NE - 1),
                        skip_group_check=True), None)
                    P.do("tensor", lambda e, j=j, p=p: e.matmul(
                        st_q[0:1, 0:TSL], ones_b[:, :], fsq[p][:, :],
                        start=(j == 0), stop=(j == NE - 1),
                        skip_group_check=True), None)
                    stat4[j] = P.mark("tensor", "st")
                down_used[0] = dmm[NE - 2]
                down_used[1] = dmm[NE - 1]
                SL[s]["stat4"] = stat4
                SL[s]["comb_v"] = comb_v
                return stat4[NE - 1]

            def emit_F(s):
                stat4 = SL[s]["stat4"]
                P.wait("vector", "st", stat4[NE - 1])
                rd4, nd4 = ln_statmath(st_s[0:1, 0:TSL], st_q[0:1, 0:TSL],
                                       r4a, r4b, scr4, TSL)
                c4a = bcast(r4b, bc4r, misc, TSL,
                            pre_t=(("ss", rd4), ("sv", misc_free)))
                c4b = bcast(r4a, bc4n, misc, TSL,
                            pre_t=(("sv", nd4), ("sv", c4a)))
                SL[s]["c4"] = (c4a, c4b)
                return c4b

            def emit_G(s):
                c0 = s * TSL
                c4a, c4b = SL[s]["c4"]
                hr_m = SL[s]["hr_m"]
                xl2 = {}
                for k in range(2):
                    P.wait("sync", "sv", hr_m[NRI - 2 + k])
                    xl2[k] = P.dma("sync", silu_s[k][:, :],
                                   xres[k * 128:(k + 1) * 128, c0:c0 + TSL],
                                   "sgx1" if k == 0 else "sgx2")
                P.wait("vector", "sv", c4b)
                P.wait("vector", "sv", c4a)
                om = {}
                for j in range(NE):
                    p = j % 2
                    ch = "sgx1" if p == 0 else "sgx2"
                    och = "sxo0" if p == 0 else "sxo1"
                    P.wait("vector", ch, xl2[j])
                    P.wait("vector", och, out_free[p])
                    P.do("vector", lambda e, j=j, p=p: e.tensor_tensor(
                        out_b[p][:, :], Fc(j), bc4r[:, :], ALU.mult), None)
                    P.do("vector", lambda e, p=p: e.tensor_tensor(
                        out_b[p][:, :], out_b[p][:, :], bc4n[:, :], ALU.add), None)
                    om[j] = P.do("vector", lambda e, p=p: e.tensor_tensor(
                        out_b[p][:, :], out_b[p][:, :], silu_s[p][:, :], ALU.add), "sv")
                    if j + 2 < NE:
                        P.wait("sync", "sv", om[j])
                        xl2[j + 2] = P.dma("sync", silu_s[p][:, :],
                                           xres[(j + 2) * 128:(j + 3) * 128, c0:c0 + TSL],
                                           ch)
                    P.wait("sync", "sv", om[j])
                    out_free[p] = P.dma(
                        "sync", outT[j * 128:(j + 1) * 128, c0:c0 + TSL],
                        out_b[p][:, :], och)
                om_prev[0] = om[NE - 2]
                om_prev[1] = om[NE - 1]

            emit_A(0)
            misc_free = emit_B(0)
            if dbg:
                P.wait("sync", "sv", misc_free)
                P.dma("sync", dbg_t["d_ln2"], ln2_sl[:, :], "sxd")
                P.dma("sync", dbg_t["d_comb"], comb_sb[:, :], "sxd")
            emit_C(0)
            ln2_used = emit_D(0)
            for s in range(NSL):
                H_used = emit_E(s)
                if s + 1 < NSL:
                    emit_A(s + 1)
                    misc_free = emit_B(s + 1)
                misc_free = emit_F(s)
                emit_G(s)
                if dbg and s == 0:
                    P.wait("sync", "sv", SL[0]["comb_v"][NE - 1])
                    P.dma("sync", dbg_t["d_h2"], B[:, 0:2 * TSL], "sxd")
                    P.dma("sync", dbg_t["d_ffn"], ffn_sl[:, :], "sxd")
                if s + 1 < NSL:
                    emit_C(s + 1)
                    ln2_used = emit_D(s + 1)
            P.wait("sync", "sxo0", P.cnt["sxo0"])
            P.wait("sync", "sxo1", P.cnt["sxo1"])
            P.wait("sync", "sxw", P.cnt["sxw"])
            if dbg:
                P.wait("sync", "sxd", P.cnt["sxd"])
            P.flush("ffn")

    return nc



def _prep_km(W):
    K, M = W.shape
    return np.ascontiguousarray(
        W.reshape(K // 128, 128, M // 128, 128).transpose(2, 1, 0, 3)
         .reshape(M // 128, 128, K))


def _common_inputs(W_qkv, b_qkv, W_out, b_out, router_weight,
                   sh_gate, sh_up, sh_down, r_gate, r_up, r_down):
    f32 = np.float32
    return {
        "wqkv": _prep_km(np.asarray(W_qkv, f32)),
        "bqkv": np.ascontiguousarray(np.asarray(b_qkv, f32).reshape(48, 128).T),
        "wout": _prep_km(np.asarray(W_out, f32)),
        "bout": np.ascontiguousarray(np.asarray(b_out, f32).reshape(16, 128).T),
        "routr": np.ascontiguousarray(
            np.asarray(router_weight, f32).T.reshape(16, 128, 8)
            .transpose(1, 0, 2).reshape(128, 128)),
        "wsg": _prep_km(np.ascontiguousarray(np.asarray(sh_gate, f32).T)),
        "wsu": _prep_km(np.ascontiguousarray(np.asarray(sh_up, f32).T)),
        "wsd": _prep_km(np.ascontiguousarray(np.asarray(sh_down, f32).T)),
        "wrg": _prep_km(np.ascontiguousarray(np.asarray(r_gate, f32).T)),
        "wru": _prep_km(np.ascontiguousarray(np.asarray(r_up, f32).T)),
        "wrd": _prep_km(np.ascontiguousarray(np.asarray(r_down, f32).T)),
        "identb": np.eye(128, dtype=ml_dtypes.bfloat16),
        "identf": np.eye(128, dtype=f32),
        "causal": np.where(np.tril(np.ones((128, 128), dtype=bool)),
                           np.float32(0.0), np.float32(NEG)).astype(f32),
    }


def kernel(L0_attn_inX, L0_attn_W_qkv, L0_attn_b_qkv, L0_attn_W_out,
           L0_attn_b_out, L0_ffn_router_weight, L0_ffn_sh_gate, L0_ffn_sh_up,
           L0_ffn_sh_down, L0_ffn_r_gate, L0_ffn_r_up, L0_ffn_r_down):
    S = 2048
    ncores = 8
    x = np.asarray(L0_attn_inX, np.float32)
    cm = _common_inputs(L0_attn_W_qkv, L0_attn_b_qkv, L0_attn_W_out,
                        L0_attn_b_out, L0_ffn_router_weight, L0_ffn_sh_gate,
                        L0_ffn_sh_up, L0_ffn_sh_down, L0_ffn_r_gate,
                        L0_ffn_r_up, L0_ffn_r_down)
    in_maps = []
    for c in range(ncores):
        m = dict(cm)
        m["xT"] = np.ascontiguousarray(x[c].T)
        in_maps.append(m)
    nc = build_program(S)
    res = run_bass_kernel_spmd(nc, in_maps, core_ids=list(range(ncores)))
    out = np.stack([np.asarray(r["outT"]).T for r in res.results], axis=0)
    return np.ascontiguousarray(out.astype(np.float32))


# version 19
# speedup vs baseline: 1.0830x; 1.0830x over previous
import math
from contextlib import ExitStack

import numpy as np
import ml_dtypes
import concourse.bass as bass
import concourse.mybir as mybir
from concourse.bass_utils import run_bass_kernel_spmd

F32 = mybir.dt.float32
BF16 = mybir.dt.bfloat16
AF = mybir.ActivationFunctionType
ALU = mybir.AluOpType
AX = mybir.AxisListType

E = 2048
NE = E // 128
H = 16
HD = 128
INTER = 8192
NI = INTER // 128
RINTER = 1024
NRI = RINTER // 128
SCALE = 1.0 / math.sqrt(HD)
EPS = 1e-5
NEG = -1.0e30

ENGINES = ("tensor", "vector", "scalar", "gpsimd", "sync")


def _splits(total, step):
    return [(o, min(step, total - o)) for o in range(0, total, step)]


class Prog:

    def __init__(self, nc, ctx):
        self.nc = nc
        self.q = {e: [] for e in ENGINES}
        self.sems = {}
        self.cnt = {}
        for name in ("st", "sv", "ss", "sge", "sgc", "sga", "sgb", "sgs",
                     "sgw0", "sgw1", "sya", "syb",
                     "sgx0", "sgx1", "sgx2", "sgx3",
                     "sxa", "sxb", "sxw", "sxo0", "sxo1", "sxd"):
            self.sems[name] = ctx.enter_context(nc.semaphore(name))
            self.cnt[name] = 0

    def do(self, eng, fn, sem=None, k=1):
        if sem is None:
            self.q[eng].append(fn)
            return None
        self.cnt[sem] += k
        m = self.cnt[sem]
        h = self.sems[sem]
        self.q[eng].append(lambda e, fn=fn, h=h, k=k: fn(e).then_inc(h, k))
        return m

    def wait(self, eng, sem, val):
        if val is None or val <= 0:
            return
        assert val <= self.cnt[sem], (sem, val, self.cnt[sem])
        h = self.sems[sem]
        self.q[eng].append(lambda e, h=h, v=val: e.wait_ge(h, v))

    def mark(self, eng, sem, k=1):
        self.cnt[sem] += k
        m = self.cnt[sem]
        h = self.sems[sem]
        prev = self.q[eng].pop()
        self.q[eng].append(lambda e, prev=prev, h=h, k=k: prev(e).then_inc(h, k))
        return m

    def dma(self, eng, out_ap, in_ap, ch):
        return self.do(eng, lambda e, o=out_ap, i=in_ap: e.dma_start(o, i),
                       ch, 16)

    def flush(self, name):
        qs = self.q
        self.q = {e: [] for e in ENGINES}
        if sum(len(v) for v in qs.values()) == 0:
            return
        with self.nc.Block(name) as blk:
            for eng in ENGINES:
                lst = qs[eng]
                if lst:
                    getattr(blk, eng)(lambda e, lst=lst: [f(e) for f in lst])


def build_program(S, dbg=False):
    nc = bass.Bass("TRN2", target_bir_lowering=False,
                   detect_race_conditions=False)
    NS = S // 128
    TSL = min(512, S)
    NSL = S // TSL
    NTT = TSL // 128

    xT = nc.dram_tensor("xT", [E, S], F32, kind="ExternalInput").ap()
    wqkv = nc.dram_tensor("wqkv", [48, 128, E], F32, kind="ExternalInput").ap()
    bqkv = nc.dram_tensor("bqkv", [128, 48], F32, kind="ExternalInput").ap()
    wout = nc.dram_tensor("wout", [16, 128, E], F32, kind="ExternalInput").ap()
    boutd = nc.dram_tensor("bout", [128, 16], F32, kind="ExternalInput").ap()
    routr = nc.dram_tensor("routr", [128, 128], F32, kind="ExternalInput").ap()
    wsg = nc.dram_tensor("wsg", [NI, 128, E], F32, kind="ExternalInput").ap()
    wsu = nc.dram_tensor("wsu", [NI, 128, E], F32, kind="ExternalInput").ap()
    wsd = nc.dram_tensor("wsd", [NE, 128, INTER], F32, kind="ExternalInput").ap()
    wrg = nc.dram_tensor("wrg", [NRI, 128, E], F32, kind="ExternalInput").ap()
    wru = nc.dram_tensor("wru", [NRI, 128, E], F32, kind="ExternalInput").ap()
    wrd = nc.dram_tensor("wrd", [NE, 128, RINTER], F32, kind="ExternalInput").ap()
    identb = nc.dram_tensor("identb", [128, 128], BF16, kind="ExternalInput").ap()
    identf = nc.dram_tensor("identf", [128, 128], F32, kind="ExternalInput").ap()
    causal = nc.dram_tensor("causal", [128, 128], F32, kind="ExternalInput").ap()
    outT = nc.dram_tensor("outT", [E, S], F32, kind="ExternalOutput").ap()
    wsgb = nc.dram_tensor("wsgb", [NI, 128, E], BF16, kind="Internal").ap()
    wsub = nc.dram_tensor("wsub", [NI, 128, E], BF16, kind="Internal").ap()
    wsdb = nc.dram_tensor("wsdb", [NE, 128, INTER], BF16, kind="Internal").ap()
    wrgb = nc.dram_tensor("wrgb", [NRI, 128, E], BF16, kind="Internal").ap()
    wrub = nc.dram_tensor("wrub", [NRI, 128, E], BF16, kind="Internal").ap()
    wrdb = nc.dram_tensor("wrdb", [NE, 128, RINTER], BF16, kind="Internal").ap()
    xres = nc.dram_tensor(
        "xres", [E, S], F32,
        kind="ExternalOutput" if dbg else "Internal").ap()

    dbg_t = {}
    if dbg:
        for nm, shp, dt in [
            ("d_ln1", [128, NE * S], BF16), ("d_q0", [128, S], BF16),
            ("d_k0", [128, S], BF16), ("d_v0", [128, S], BF16),
            ("d_p0", [128, S], BF16), ("d_av", [128, H * S], BF16),
            ("d_attn", [128, NE * S], BF16), ("d_ln2", [128, NE * TSL], BF16),
            ("d_h2", [128, 2 * TSL], BF16), ("d_ffn", [128, NE * TSL], BF16),
            ("d_comb", [128, NTT], F32), ("d_sums", [128, S // 128], F32),
            ("d_rs", [128, S // 128], F32),
        ]:
            dbg_t[nm] = nc.dram_tensor(nm, shp, dt, kind="ExternalOutput").ap()

    ctx = ExitStack()
    with ctx:
        en = ctx.enter_context
        P = Prog(nc, ctx)

        identb_sb = en(nc.sbuf_tensor("identb_sb", [128, 128], BF16))
        identf_sb = en(nc.sbuf_tensor("identf_sb", [128, 128], F32))
        causal_sb = en(nc.sbuf_tensor("causal_sb", [128, 128], F32))
        bq_sb = en(nc.sbuf_tensor("bq_sb", [128, 48], F32))
        bqs_sb = en(nc.sbuf_tensor("bqs_sb", [128, 16], F32))
        bo_sb = en(nc.sbuf_tensor("bo_sb", [128, 16], F32))
        rt_sb = en(nc.sbuf_tensor("rt_sb", [128, 128], BF16))
        ones_b = en(nc.sbuf_tensor("ones_b", [128, 1], BF16))
        ones_f = en(nc.sbuf_tensor("ones_f", [1, 128], F32))
        BW = max(NE * S, NI * TSL)
        B = en(nc.sbuf_tensor("B", [128, BW], BF16))
        bc2_r = en(nc.sbuf_tensor("bc2_r", [128, S], BF16))
        bc2_n = en(nc.sbuf_tensor("bc2_n", [128, S], BF16))

        def Bc(k):
            return B[:, k * S:(k + 1) * S]

        def ln_statmath(sum_ps, sq_ps, row_a, row_b, scr, width):
            a = row_a[0:1, 0:width]
            b = row_b[0:1, 0:width]
            s = scr[0:1, 0:width]
            P.do("vector", lambda e: e.tensor_scalar_mul(a, sum_ps, 1.0 / E), None)
            P.do("vector", lambda e: e.tensor_scalar_mul(b, sq_ps, 1.0 / E), None)
            P.do("vector", lambda e: e.tensor_tensor(s, a, a, ALU.mult), None)
            P.do("vector", lambda e: e.tensor_tensor(b, b, s, ALU.subtract), None)
            sdone = P.do("vector", lambda e: e.tensor_scalar_add(b, b, EPS), "sv")
            P.wait("scalar", "sv", sdone)
            P.do("scalar", lambda e: e.activation(b, b, AF.Ln), None)
            rdone = P.do("scalar", lambda e: e.activation(b, b, AF.Exp, scale=-0.5), "ss")
            P.wait("vector", "ss", rdone)
            ndone = P.do("vector", lambda e: e.scalar_tensor_tensor(
                a, a, -1.0, b, ALU.mult, ALU.mult), "sv")
            return rdone, ndone

        def bcast(row_sb, dst_sb, ps, width, pre_t=(), pre_v=()):
            for sem, val in pre_t:
                P.wait("tensor", sem, val)
            for (o, w) in _splits(width, 512):
                P.do("tensor", lambda e, o=o, w=w: e.matmul(
                    ps[:, o:o + w], ones_f[:, :], row_sb[0:1, o:o + w],
                    start=True, stop=True), None)
            mm = P.mark("tensor", "st")
            P.wait("vector", "st", mm)
            for sem, val in pre_v:
                P.wait("vector", sem, val)
            return P.do("vector", lambda e, w=width: e.tensor_copy(
                dst_sb[:, 0:w], ps[:, 0:w]), "sv")

        with ExitStack() as sc1:
            e1 = sc1.enter_context
            A = e1(nc.sbuf_tensor("A", [128, NE * S], BF16))
            W4 = [e1(nc.sbuf_tensor(f"W4_{s}", [128, E], BF16)) for s in range(4)]
            WCH = ("sgw0", "sgw1", "sga", "sgb")
            wstep_m = {}

            def wdma_step(n):
                h, t = divmod(n, 3)
                s = n % 4
                wstep_m[n] = P.dma("gpsimd", W4[s][:, :], wqkv[16 * t + h],
                                   WCH[s])

            def Ac(k):
                return A[:, k * S:(k + 1) * S]

            P.dma("gpsimd", identb_sb[:, :], identb, "sgc")
            P.dma("gpsimd", identf_sb[:, :], identf, "sgc")
            P.dma("gpsimd", causal_sb[:, :], causal, "sgc")
            P.dma("gpsimd", bq_sb[:, :], bqkv, "sgc")
            P.dma("gpsimd", bo_sb[:, :], boutd, "sgc")
            P.dma("gpsimd", rt_sb[:, :], routr, "sgc")
            P.do("gpsimd", lambda e: e.memset(ones_b[:, :], 1.0), "sge")
            MEMS = P.do("gpsimd", lambda e: e.memset(ones_f[:, :], 1.0), "sge")
            CONSTS = P.cnt["sgc"]
            P.wait("vector", "sgc", CONSTS)
            P.wait("vector", "sge", MEMS)
            P.do("vector", lambda e: e.tensor_scalar_mul(
                bqs_sb[:, :], bq_sb[:, 0:16], SCALE), None)

            with ExitStack() as pctx:
                pe = pctx.enter_context
                ps1 = pe(nc.psum_tensor("ps1", [128, S], F32))
                ps2 = pe(nc.psum_tensor("ps2", [128, S], F32))
                sqs = [pe(nc.sbuf_tensor(f"sqs{i}", [128, S], BF16)) for i in range(2)]
                tmp1 = pe(nc.sbuf_tensor("tmp1", [128, S], F32))
                row_a = pe(nc.sbuf_tensor("row_a", [1, S], F32))
                row_b = pe(nc.sbuf_tensor("row_b", [1, S], F32))

                xbf_m = {}
                sq_done = {}
                for k in range(min(4, NE)):
                    xbf_m[k] = P.dma("gpsimd", Bc(k),
                                     xT[k * 128:(k + 1) * 128, :], f"sgx{k % 4}")
                P.wait("tensor", "sgc", CONSTS)
                P.wait("tensor", "sge", MEMS)
                sqm = [None, None]
                last_mm = None
                for k in range(NE):
                    chan = f"sgx{k % 4}"
                    P.wait("vector", chan, xbf_m[k])
                    if sqm[k % 2] is not None:
                        P.wait("vector", "st", sqm[k % 2])
                    sq_done[k] = P.do("vector", lambda e, k=k: e.tensor_tensor(
                        sqs[k % 2][:, :], Bc(k), Bc(k), ALU.mult), "sv")
                    if k + 4 < NE:
                        P.wait("gpsimd", "sv", sq_done[k])
                        xbf_m[k + 4] = P.dma("gpsimd", Bc(k + 4),
                                             xT[(k + 4) * 128:(k + 5) * 128, :],
                                             f"sgx{(k + 4) % 4}")
                    P.wait("tensor", "sv", sq_done[k])
                    P.wait("tensor", chan, xbf_m[k])
                    for (o, w) in _splits(S, 512):
                        P.do("tensor", lambda e, o=o, w=w, k=k: e.matmul(
                            ps1[0:1, o:o + w], ones_b[:, :], Bc(k)[:, o:o + w],
                            start=(k == 0), stop=(k == NE - 1),
                            skip_group_check=True), None)
                    for (o, w) in _splits(S, 512):
                        P.do("tensor", lambda e, o=o, w=w, k=k, p=k % 2: e.matmul(
                            ps2[0:1, o:o + w], ones_b[:, :], sqs[p][:, o:o + w],
                            start=(k == 0), stop=(k == NE - 1),
                            skip_group_check=True), None)
                    last_mm = P.mark("tensor", "st")
                    sqm[k % 2] = last_mm
                P.wait("vector", "st", last_mm)
                rdone, ndone = ln_statmath(ps1[0:1, :], ps2[0:1, :],
                                           row_a, row_b, tmp1, S)
                P.wait("tensor", "ss", rdone)
                for (o, w) in _splits(S, 512):
                    P.do("tensor", lambda e, o=o, w=w: e.matmul(
                        ps1[:, o:o + w], ones_f[:, :], row_b[0:1, o:o + w],
                        start=True, stop=True), None)
                P.wait("tensor", "sv", ndone)
                for (o, w) in _splits(S, 512):
                    P.do("tensor", lambda e, o=o, w=w: e.matmul(
                        ps2[:, o:o + w], ones_f[:, :], row_a[0:1, o:o + w],
                        start=True, stop=True), None)
                bc_mm = P.mark("tensor", "st")
                P.wait("vector", "st", bc_mm)
                for k in range(NE):
                    P.do("vector", lambda e, k=k: e.tensor_tensor(
                        tmp1[:, :], Bc(k), ps1[:, :], ALU.mult), None)
                    P.do("vector", lambda e, k=k: e.tensor_tensor(
                        Ac(k), tmp1[:, :], ps2[:, :], ALU.add), None)
                for n in range(min(4, 3 * H)):
                    wdma_step(n)
                if dbg:
                    mm = P.mark("vector", "sv")
                    P.wait("sync", "sv", mm)
                    P.dma("sync", dbg_t["d_ln1"], A[:, :], "sxd")
                P.flush("init_ln1")

            with ExitStack() as actx:
                ae = actx.enter_context
                apsum = ExitStack()
                ps_sc = apsum.enter_context(nc.psum_tensor("ps_sc", [128, S], F32))
                pt_ps = [apsum.enter_context(nc.psum_tensor(f"pt{i}", [128, 1024], BF16)) for i in range(2)]
                av_ps = apsum.enter_context(nc.psum_tensor("av_ps", [128, 512], F32))
                qk_ps = apsum.enter_context(nc.psum_tensor("qk_ps", [128, 512], F32))
                Qp = [ae(nc.sbuf_tensor(f"Qp{i}", [128, S], BF16)) for i in range(2)]
                Kp = [ae(nc.sbuf_tensor(f"Kp{i}", [128, S], BF16)) for i in range(2)]
                Vb = ae(nc.sbuf_tensor("Vb", [128, S], BF16))
                VT = ae(nc.sbuf_tensor("VT", [128, S], BF16))
                Pb = [ae(nc.sbuf_tensor(f"Pb{i}", [128, S], BF16)) for i in range(4)]
                pt_sb = [ae(nc.sbuf_tensor(f"ptsb{i}", [128, 1024], BF16)) for i in range(2)]
                sums = ae(nc.sbuf_tensor("sums", [128, NS], F32))
                rs = ae(nc.sbuf_tensor("rs", [128, NS], F32))
                negmx = ae(nc.sbuf_tensor("negmx", [128, 1], F32))
                mx4 = ae(nc.sbuf_tensor("mx4", [128, 4], F32))

                NOSL = len(_splits(S, 512))
                NG = 3 * NOSL

                gmm = {}
                gevac = {}

                def qkv_subgroup(h, g):
                    t, oi = divmod(g, NOSL)
                    o, w = _splits(S, 512)[oi]
                    n = 3 * h + t
                    s = n % 4
                    par = h % 2
                    P.wait("tensor", WCH[s], wstep_m[n])
                    prev = gevac.get((h, g - 1)) or gevac.get((h - 1, NG - 1))
                    P.wait("tensor", "ss", prev)
                    for k in range(NE):
                        P.do("tensor", lambda e, s=s, o=o, w=w, k=k: e.matmul(
                            qk_ps[:, 0:w], W4[s][:, k * 128:(k + 1) * 128],
                            Ac(k)[:, o:o + w], start=(k == 0),
                            stop=(k == NE - 1), skip_group_check=True), None)
                    gmm[(h, g)] = P.mark("tensor", "st")
                    if oi == NOSL - 1 and n + 4 < 3 * H:
                        P.wait("gpsimd", "st", gmm[(h, g)])
                        wdma_step(n + 4)
                    P.wait("scalar", "st", gmm[(h, g)])
                    if t == 0:
                        dst, bias_ap, sc = Qp[par], bqs_sb[:, h:h + 1], SCALE
                    elif t == 1:
                        dst, bias_ap, sc = Kp[par], bq_sb[:, 16 + h:17 + h], 1.0
                    else:
                        dst, bias_ap, sc = Vb, bq_sb[:, 32 + h:33 + h], 1.0
                    gevac[(h, g)] = P.do(
                        "scalar", lambda e, dst=dst, o=o, w=w, b=bias_ap, sc=sc: e.activation(
                            dst[:, o:o + w], qk_ps[:, 0:w], AF.Identity,
                            bias=b, scale=sc), "ss")

                stage_jobs = []
                for i in range(0, NI, 8):
                    stage_jobs.append((wsgb[i:i + 8], wsg[i:i + 8]))
                for i in range(0, NI, 8):
                    stage_jobs.append((wsub[i:i + 8], wsu[i:i + 8]))
                for i in range(0, NE, 2):
                    stage_jobs.append((wsdb[i:i + 2], wsd[i:i + 2]))
                stage_jobs.append((wrgb[:], wrg[:]))
                stage_jobs.append((wrub[:], wru[:]))
                stage_jobs.append((wrdb[:], wrd[:]))
                nstg = (len(stage_jobs) + H - 1) // H

                head_exp = None
                for h in range(H):
                    par = h % 2
                    for dst, srcap in stage_jobs[h * nstg:(h + 1) * nstg]:
                        P.dma("gpsimd", dst, srcap, "sgs")
                    if h == 0:
                        for g in range(NG):
                            qkv_subgroup(0, g)
                    slot = 0
                    P.wait("tensor", "ss", gevac[(h, NG - 1)])
                    vt_cast = {}
                    n_g4 = (NS + 3) // 4
                    for g4 in range(n_g4):
                        kts = list(range(4 * g4, min(4 * g4 + 4, NS)))
                        if g4 >= 2:
                            P.wait("tensor", "ss", vt_cast[g4 - 2])
                        for j, kt in enumerate(kts):
                            P.do("tensor", lambda e, j=j, kt=kt, g4=g4: e.transpose(
                                pt_ps[g4 % 2][:, j * 128:(j + 1) * 128],
                                Vb[:, kt * 128:(kt + 1) * 128], identb_sb[:, :]), None)
                        tg = P.mark("tensor", "st")
                        P.wait("scalar", "st", tg)
                        w = 128 * len(kts)
                        vt_cast[g4] = P.do("scalar", lambda e, g4=g4, w=w, o=4 * g4 * 128: e.copy(
                            VT[:, o:o + w], pt_ps[g4 % 2][:, 0:w]), "ss")
                    vt_done = P.cnt["ss"]

                    prev_exp = vt_done
                    av_evac = None
                    for qb0 in range(0, NS, 4):
                        nqt = min(4, NS - qb0)
                        Lb = (qb0 + nqt) * 128
                        pscale_m = {}
                        for qt in range(nqt):
                            qg = qb0 + qt
                            Lq = (qg + 1) * 128
                            P.wait("tensor", "ss", prev_exp)
                            sls = _splits(Lq, 512)
                            split = Lq > 1024
                            lm = None
                            for i, (o, w) in enumerate(sls):
                                P.do("tensor", lambda e, o=o, w=w, qg=qg, par=par: e.matmul(
                                    ps_sc[:, o:o + w],
                                    Qp[par][:, qg * 128:(qg + 1) * 128],
                                    Kp[par][:, o:o + w], start=True, stop=True,
                                    skip_group_check=True), None)
                                smm = P.mark("tensor", "st")
                                if not split and i < len(sls) - 1:
                                    continue
                                P.wait("vector", "st", smm)
                                d0 = qg * 128
                                if o <= d0 < o + w:
                                    P.do("vector", lambda e, d0=d0: e.tensor_tensor(
                                        ps_sc[:, d0:d0 + 128],
                                        ps_sc[:, d0:d0 + 128],
                                        causal_sb[:, :], ALU.add), None)
                                if split:
                                    lm = P.do("vector", lambda e, i=i, o=o, w=w: e.reduce_max(
                                        mx4[:, i:i + 1], ps_sc[:, o:o + w],
                                        axis=AX.X), "sv")
                            if split:
                                P.wait("vector", "sv", lm)
                                mx = P.do("vector", lambda e, ns=len(sls): e.reduce_max(
                                    negmx[:, :], mx4[:, 0:ns], axis=AX.X,
                                    negate=True), "sv")
                            else:
                                mx = P.do("vector", lambda e, Lq=Lq: e.reduce_max(
                                    negmx[:, :], ps_sc[:, 0:Lq], axis=AX.X,
                                    negate=True), "sv")
                            if h + 1 < H and slot < NG and qg >= NS - NG:
                                qkv_subgroup(h + 1, slot)
                                slot += 1
                            P.wait("scalar", "sv", mx)
                            prev_exp = P.do("scalar", lambda e, qt=qt, Lq=Lq, qg=qg: e.activation(
                                Pb[qt][:, 0:Lq], ps_sc[:, 0:Lq], AF.Exp,
                                bias=negmx[:, :], scale=1.0,
                                accum_out=sums[:, qg:qg + 1]), "ss")
                            P.wait("vector", "ss", prev_exp)
                            rm = P.do("vector", lambda e, qg=qg: e.reciprocal(
                                rs[:, qg:qg + 1], sums[:, qg:qg + 1]), "sv")
                            P.wait("vector", "sv", rm)
                            P.do("vector", lambda e, qt=qt, Lq=Lq, qg=qg: e.tensor_scalar_mul(
                                Pb[qt][:, 0:Lq], Pb[qt][:, 0:Lq],
                                rs[:, qg:qg + 1]), None)
                            if Lq < Lb:
                                P.do("vector", lambda e, qt=qt, Lq=Lq, Lb=Lb: e.memset(
                                    Pb[qt][:, Lq:Lb], 0.0), None)
                            pscale_m[qt] = P.mark("vector", "sv")
                        cast_m = {}
                        av_mm = {}
                        nkt = Lb // 128
                        nbt = (nkt + 1) // 2
                        wqt = 128 * nqt
                        for bt in range(nbt):
                            kts = [k for k in (2 * bt, 2 * bt + 1) if k < nkt]
                            pp = bt % 2
                            P.wait("tensor", "sv", pscale_m[nqt - 1])
                            if bt >= 2:
                                P.wait("tensor", cast_m[bt - 2][0], cast_m[bt - 2][1])
                            if bt == 0 and av_evac is not None:
                                P.wait("tensor", "ss", av_evac)
                            for j, kt in enumerate(kts):
                                for qt in range(nqt):
                                    P.do("tensor", lambda e, qt=qt, kt=kt, pp=pp, j=j: e.transpose(
                                        pt_ps[pp][:, j * wqt + qt * 128:j * wqt + (qt + 1) * 128],
                                        Pb[qt][:, kt * 128:(kt + 1) * 128],
                                        identb_sb[:, :]), None)
                            tg = P.mark("tensor", "st")
                            ceng, csem = (("vector", "sv") if bt % 2 else ("scalar", "ss"))
                            P.wait(ceng, "st", tg)
                            if bt >= 1:
                                P.wait(ceng, "st", av_mm[bt - 1])
                            cast_m[bt] = (csem, P.do(ceng, lambda e, pp=pp, w=len(kts) * wqt: (
                                e.copy if hasattr(e, "copy") else e.tensor_copy)(
                                pt_sb[pp][:, 0:w], pt_ps[pp][:, 0:w]), csem))
                            P.wait("tensor", cast_m[bt][0], cast_m[bt][1])
                            for j, kt in enumerate(kts):
                                mm = P.do("tensor", lambda e, pp=pp, kt=kt, j=j, w=wqt, nkt=nkt: e.matmul(
                                    av_ps[:, 0:w], VT[:, kt * 128:(kt + 1) * 128],
                                    pt_sb[pp][:, j * w:(j + 1) * w], start=(kt == 0),
                                    stop=(kt == nkt - 1), skip_group_check=True),
                                    "st" if kt == kts[-1] else None)
                            av_mm[bt] = mm
                        P.wait("scalar", "st", av_mm[nbt - 1])
                        av_evac = P.do("scalar", lambda e, h=h, qb0=qb0, w=128 * nqt: e.copy(
                            Bc(h)[:, qb0 * 128:qb0 * 128 + w], av_ps[:, 0:w]), "ss")
                    if h + 1 < H:
                        while slot < NG:
                            qkv_subgroup(h + 1, slot)
                            slot += 1
                    head_exp = prev_exp
                if dbg:
                    P.wait("sync", "ss", P.cnt["ss"])
                    P.wait("sync", "sv", P.cnt["sv"])
                    P.dma("sync", dbg_t["d_sums"], sums[:, :], "sxd")
                    P.dma("sync", dbg_t["d_rs"], rs[:, :], "sxd")
                    P.dma("sync", dbg_t["d_q0"], Qp[(H - 1) % 2][:, :], "sxd")
                    P.dma("sync", dbg_t["d_k0"], Kp[(H - 1) % 2][:, :], "sxd")
                    P.dma("sync", dbg_t["d_v0"], Vb[:, :], "sxd")
                    P.dma("sync", dbg_t["d_p0"], Pb[0][:, :], "sxd")
                    P.dma("sync", dbg_t["d_av"], B[:, 0:H * S], "sxd")
                P.flush("attn")
                apsum.close()

                with ExitStack() as octx:
                    oe = octx.enter_context
                    wo_ps = [oe(nc.psum_tensor(f"wo{i}", [128, S], F32)) for i in range(2)]
                    wo_dma = {0: P.dma("gpsimd", W4[0][:, :], wout[0], "sga"),
                              1: P.dma("gpsimd", W4[1][:, :], wout[1], "sgb")}
                    evj = {}
                    for j in range(NE):
                        p = j % 2
                        P.wait("tensor", "sga" if p == 0 else "sgb", wo_dma[j])
                        if j >= 2:
                            P.wait("tensor", "ss", evj[j - 2])
                        for (o, w) in _splits(S, 512):
                            for k in range(NE):
                                P.do("tensor", lambda e, o=o, w=w, k=k, p=p: e.matmul(
                                    wo_ps[p][:, o:o + w],
                                    W4[p][:, k * 128:(k + 1) * 128],
                                    Bc(k)[:, o:o + w],
                                    start=(k == 0), stop=(k == NE - 1),
                                    skip_group_check=True), None)
                        g = P.mark("tensor", "st")
                        if j + 2 < NE:
                            P.wait("gpsimd", "st", g)
                            wo_dma[j + 2] = P.dma("gpsimd", W4[p][:, :],
                                                  wout[j + 2],
                                                  "sga" if p == 0 else "sgb")
                        P.wait("scalar", "st", g)
                        evj[j] = P.do("scalar", lambda e, j=j, p=p: e.activation(
                            Ac(j)[:, :], wo_ps[p][:, :], AF.Identity,
                            bias=bo_sb[:, j:j + 1], scale=1.0), "ss")
                    if dbg:
                        P.wait("sync", "ss", P.cnt["ss"])
                        P.dma("sync", dbg_t["d_attn"], A[:, :], "sxd")
                    P.flush("wout")

            with ExitStack() as pctx:
                pe = pctx.enter_context
                ps1 = pe(nc.psum_tensor("p35_1", [128, S], F32))
                ps2 = pe(nc.psum_tensor("p35_2", [128, S], F32))
                sqs = [pe(nc.sbuf_tensor(f"p35sq{i}", [128, S], BF16)) for i in range(2)]
                xf = [pe(nc.sbuf_tensor(f"p35x{i}", [128, S], F32)) for i in range(2)]
                xr = pe(nc.sbuf_tensor("p35r", [128, S], F32))
                xrb = pe(nc.sbuf_tensor("p35rb", [128, S], BF16))
                row_a = pe(nc.sbuf_tensor("row_a2", [1, S], F32))
                row_b = pe(nc.sbuf_tensor("row_b2", [1, S], F32))

                sqm = [None, None]
                last_mm = None
                for k in range(NE):
                    if sqm[k % 2] is not None:
                        P.wait("vector", "st", sqm[k % 2])
                    sq_done = P.do("vector", lambda e, k=k: e.tensor_tensor(
                        sqs[k % 2][:, :], Ac(k), Ac(k), ALU.mult), "sv")
                    P.wait("tensor", "sv", sq_done)
                    for (o, w) in _splits(S, 512):
                        P.do("tensor", lambda e, o=o, w=w, k=k: e.matmul(
                            ps1[0:1, o:o + w], ones_b[:, :], Ac(k)[:, o:o + w],
                            start=(k == 0), stop=(k == NE - 1),
                            skip_group_check=True), None)
                    for (o, w) in _splits(S, 512):
                        P.do("tensor", lambda e, o=o, w=w, k=k, p=k % 2: e.matmul(
                            ps2[0:1, o:o + w], ones_b[:, :], sqs[p][:, o:o + w],
                            start=(k == 0), stop=(k == NE - 1),
                            skip_group_check=True), None)
                    last_mm = P.mark("tensor", "st")
                    sqm[k % 2] = last_mm
                P.wait("vector", "st", last_mm)
                rdone, ndone = ln_statmath(ps1[0:1, :], ps2[0:1, :],
                                           row_a, row_b, xr, S)
                bcr_m = bcast(row_b, bc2_r, ps1, S, pre_t=(("ss", rdone),))
                bcn_m = bcast(row_a, bc2_n, ps2, S, pre_t=(("sv", ndone),))

                xld = {0: P.dma("sync", xf[0][:, :], xT[0:128, :], "sxa"),
                       1: P.dma("sync", xf[1][:, :], xT[128:256, :], "sxb")}
                wr_m = {}
                cast_m = {}
                stat_m = {}
                sq_v = {}
                P.wait("vector", "sv", bcn_m)
                P.wait("vector", "sv", bcr_m)
                for k in range(NE):
                    p = k % 2
                    ch = "sxa" if p == 0 else "sxb"
                    P.wait("vector", ch, xld[k])
                    if k >= 1:
                        P.wait("vector", "sxw", wr_m[k - 1])
                        P.wait("vector", "ss", cast_m[k - 1])
                    P.do("vector", lambda e, k=k: e.tensor_tensor(
                        xr[:, :], Ac(k), bc2_r[:, :], ALU.mult), None)
                    P.do("vector", lambda e: e.tensor_tensor(
                        xr[:, :], xr[:, :], bc2_n[:, :], ALU.add), None)
                    vm = P.do("vector", lambda e, p=p: e.tensor_tensor(
                        xr[:, :], xr[:, :], xf[p][:, :], ALU.add), "sv")
                    if k + 2 < NE:
                        P.wait("sync", "sv", vm)
                        xld[k + 2] = P.dma("sync", xf[p][:, :],
                                           xT[(k + 2) * 128:(k + 3) * 128, :], ch)
                    P.wait("sync", "sv", vm)
                    wr_m[k] = P.dma("sync", xres[k * 128:(k + 1) * 128, :],
                                    xr[:, :], "sxw")
                    P.wait("scalar", "sv", vm)
                    if k >= 1:
                        P.wait("scalar", "sv", sq_v[k - 1])
                        P.wait("scalar", "st", stat_m[k - 1])
                    cast_m[k] = P.do("scalar", lambda e: e.copy(xrb[:, :], xr[:, :]), "ss")
                    P.wait("vector", "ss", cast_m[k])
                    if k >= 2:
                        P.wait("vector", "st", stat_m[k - 2])
                    sq_v[k] = P.do("vector", lambda e, p=p: e.tensor_tensor(
                        sqs[p][:, :], xrb[:, :], xrb[:, :], ALU.mult), "sv")
                    P.wait("tensor", "ss", cast_m[k])
                    P.wait("tensor", "sv", sq_v[k])
                    for (o, w) in _splits(S, 512):
                        P.do("tensor", lambda e, o=o, w=w, k=k: e.matmul(
                            ps1[0:1, o:o + w], ones_b[:, :], xrb[:, o:o + w],
                            start=(k == 0), stop=(k == NE - 1),
                            skip_group_check=True), None)
                    for (o, w) in _splits(S, 512):
                        P.do("tensor", lambda e, o=o, w=w, k=k, p=p: e.matmul(
                            ps2[0:1, o:o + w], ones_b[:, :], sqs[p][:, o:o + w],
                            start=(k == 0), stop=(k == NE - 1),
                            skip_group_check=True), None)
                    stat_m[k] = P.mark("tensor", "st")
                P.wait("vector", "st", stat_m[NE - 1])
                P.wait("vector", "sxw", wr_m[NE - 1])
                P.wait("vector", "ss", cast_m[NE - 1])
                rdone, ndone = ln_statmath(ps1[0:1, :], ps2[0:1, :],
                                           row_a, row_b, xr, S)
                bcast(row_b, bc2_r, ps1, S, pre_t=(("ss", rdone),))
                bcast(row_a, bc2_n, ps2, S, pre_t=(("sv", ndone),))
                P.flush("p35")

        with ExitStack() as fctx:
            fe = fctx.enter_context
            gps = [fe(nc.psum_tensor(f"gps{i}", [128, 512], F32)) for i in range(2)]
            ups = [fe(nc.psum_tensor(f"ups{i}", [128, 512], F32)) for i in range(2)]
            st_s = fe(nc.psum_tensor("st_s", [128, 512], F32))
            st_q = fe(nc.psum_tensor("st_q", [128, 512], F32))
            misc = fe(nc.psum_tensor("misc", [128, 512], F32))
            ln2_sl = fe(nc.sbuf_tensor("ln2_sl", [128, NE * TSL], BF16))
            Hr = fe(nc.sbuf_tensor("Hr", [128, NRI * TSL], BF16))
            ffn_sl = fe(nc.sbuf_tensor("ffn_sl", [128, NE * TSL], BF16))
            wg_b = [fe(nc.sbuf_tensor(f"wgb{i}", [128, E], BF16)) for i in range(2)]
            wu_b = [fe(nc.sbuf_tensor(f"wub{i}", [128, E], BF16)) for i in range(2)]
            wd_b = [fe(nc.sbuf_tensor(f"wdb{i}", [128, INTER], BF16)) for i in range(2)]
            wrd_b = [fe(nc.sbuf_tensor(f"wrdb{i}", [128, RINTER], BF16)) for i in range(2)]
            silu_s = [fe(nc.sbuf_tensor(f"silu{i}", [128, TSL], F32)) for i in range(2)]
            xp_b = [fe(nc.sbuf_tensor(f"xpb{i}", [128, TSL], F32)) for i in range(2)]
            fsq = [fe(nc.sbuf_tensor(f"fsq{i}", [128, TSL], BF16)) for i in range(2)]
            out_b = [fe(nc.sbuf_tensor(f"outb{i}", [128, TSL], F32)) for i in range(2)]
            lg_sb = fe(nc.sbuf_tensor("lg_sb", [128, 8 * NTT], F32))
            t8 = fe(nc.sbuf_tensor("t8", [128, 8], F32))
            eqm = fe(nc.sbuf_tensor("eqm", [128, 8], F32))
            v1c = fe(nc.sbuf_tensor("v1c", [128, 1], F32))
            dcol = fe(nc.sbuf_tensor("dcol", [128, 1], F32))
            ecol = fe(nc.sbuf_tensor("ecol", [128, 1], F32))
            dencol = fe(nc.sbuf_tensor("dencol", [128, 1], F32))
            comb_sb = fe(nc.sbuf_tensor("comb_sb", [128, NTT], F32))
            bc_cmb = fe(nc.sbuf_tensor("bc_cmb", [128, TSL], BF16))
            bc4r = fe(nc.sbuf_tensor("bc4r", [128, TSL], F32))
            bc4n = fe(nc.sbuf_tensor("bc4n", [128, TSL], F32))
            r4a = fe(nc.sbuf_tensor("r4a", [1, TSL], F32))
            r4b = fe(nc.sbuf_tensor("r4b", [1, TSL], F32))
            scr4 = fe(nc.sbuf_tensor("scr4", [1, TSL], F32))
            comb_row = fe(nc.sbuf_tensor("comb_row", [1, TSL], F32))

            def Hc(m):
                return B[:, m * TSL:(m + 1) * TSL]

            def L2c(k):
                return ln2_sl[:, k * TSL:(k + 1) * TSL]

            def Hrc(m):
                return Hr[:, m * TSL:(m + 1) * TSL]

            def Fc(j):
                return ffn_sl[:, j * TSL:(j + 1) * TSL]

            def pch(p):
                return "sga" if p == 0 else "sgb"

            def syp(p):
                return "sya" if p == 0 else "syb"

            P.wait("sync", "sgs", P.cnt["sgs"])
            P.dma("sync", wg_b[0][:, :], wsgb[0], "sya")
            gu0 = P.dma("sync", wu_b[0][:, :], wsub[0], "sya")
            P.dma("sync", wg_b[1][:, :], wsgb[1], "syb")
            gu1 = P.dma("sync", wu_b[1][:, :], wsub[1], "syb")
            gu_m = {0: gu0, 1: gu1}

            ln2_used = None
            H_used = None
            down_used = {0: None, 1: None}
            out_free = {0: None, 1: None}
            misc_free = None
            xp_prev = {0: None, 1: None}
            hr_prev = {0: None, 1: None}
            om_prev = {0: None, 1: None}
            SL = {}

            def emit_A(s):
                c0 = s * TSL
                xl = {}
                for k in range(2):
                    if xp_prev[k]:
                        P.wait("sync", xp_prev[k][0], xp_prev[k][1])
                    xl[k] = P.dma("sync", xp_b[k][:, :],
                                  xres[k * 128:(k + 1) * 128, c0:c0 + TSL],
                                  "sxa" if k == 0 else "sxb")
                ln2_v = {}
                for k in range(NE):
                    p = k % 2
                    ch = "sxa" if p == 0 else "sxb"
                    P.wait("gpsimd", ch, xl[k])
                    if ln2_used is not None and k == 0:
                        P.wait("gpsimd", "st", ln2_used)
                        if dbg:
                            P.wait("gpsimd", "sxd", P.cnt["sxd"])
                    P.do("gpsimd", lambda e, p=p, c0=c0: e.tensor_tensor(
                        xp_b[p][:, :], xp_b[p][:, :], bc2_r[:, c0:c0 + TSL],
                        ALU.mult), None)
                    ln2_v[k] = P.do("gpsimd", lambda e, k=k, p=p, c0=c0: e.tensor_tensor(
                        L2c(k), xp_b[p][:, :], bc2_n[:, c0:c0 + TSL], ALU.add), "sge")
                    if k + 2 < NE:
                        P.wait("sync", "sge", ln2_v[k])
                        xl[k + 2] = P.dma("sync", xp_b[p][:, :],
                                          xres[(k + 2) * 128:(k + 3) * 128, c0:c0 + TSL],
                                          ch)
                xp_prev[0] = ("sge", ln2_v[NE - 2])
                xp_prev[1] = ("sge", ln2_v[NE - 1])
                SL[s] = {"ln2_done": ("sge", ln2_v[NE - 1])}

            def emit_B(s):
                nonlocal_misc = misc_free
                ln2_done = SL[s]["ln2_done"]
                lg_read = nonlocal_misc
                for tt in range(NTT):
                    P.wait("tensor", ln2_done[0], ln2_done[1])
                    P.wait("tensor", "sv", lg_read)
                    for k in range(NE):
                        P.do("tensor", lambda e, tt=tt, k=k: e.matmul(
                            misc[:, 0:8], L2c(k)[:, tt * 128:(tt + 1) * 128],
                            rt_sb[:, k * 8:(k + 1) * 8],
                            start=(k == 0), stop=(k == NE - 1),
                            skip_group_check=True), None)
                    lgm = P.mark("tensor", "st")
                    P.wait("vector", "st", lgm)
                    lg_read = P.do("vector", lambda e, tt=tt: e.tensor_copy(
                        lg_sb[:, tt * 8:(tt + 1) * 8], misc[:, 0:8]), "sv")
                comb_m = None
                for tt in range(NTT):
                    sl = lg_sb[:, tt * 8:(tt + 1) * 8]

                    def step(eng, fn):
                        sem = "sv" if eng == "vector" else "ss"
                        m = P.do(eng, fn, sem)
                        P.wait(eng, sem, m)
                        return m

                    step("vector", lambda e, sl=sl: e.reduce_max(
                        v1c[:, :], sl, axis=AX.X))
                    step("vector", lambda e, sl=sl: e.tensor_scalar(
                        t8[:, :], sl, v1c[:, :], None, ALU.subtract))
                    step("vector", lambda e: e.tensor_scalar(
                        eqm[:, :], t8[:, :], 0.0, NEG, ALU.is_equal, ALU.mult))
                    step("vector", lambda e: e.tensor_tensor(
                        t8[:, :], t8[:, :], eqm[:, :], ALU.add))
                    dm = step("vector", lambda e: e.reduce_max(
                        dcol[:, :], t8[:, :], axis=AX.X))
                    P.wait("scalar", "sv", dm)
                    em = step("scalar", lambda e: e.activation(
                        ecol[:, :], dcol[:, :], AF.Exp))
                    P.wait("vector", "ss", em)
                    step("vector", lambda e: e.tensor_scalar_add(
                        dencol[:, :], ecol[:, :], 1.0))
                    step("vector", lambda e: e.reciprocal(ecol[:, :], dencol[:, :]))
                    comb_m = step("vector", lambda e, tt=tt: e.tensor_tensor(
                        comb_sb[:, tt:tt + 1], dencol[:, :], ecol[:, :], ALU.mult))
                P.wait("tensor", "sv", comb_m)
                P.wait("tensor", "sv", lg_read)
                for tt in range(NTT):
                    P.do("tensor", lambda e, tt=tt: e.transpose(
                        misc[0:1, tt * 128:(tt + 1) * 128],
                        comb_sb[:, tt:tt + 1], identf_sb[:, :]), None)
                ctm = P.mark("tensor", "st")
                P.wait("vector", "st", ctm)
                crm = P.do("vector", lambda e: e.tensor_copy(
                    comb_row[:, :], misc[0:1, 0:TSL]), "sv")
                cb = bcast(comb_row, bc_cmb, misc, TSL, pre_t=(("sv", crm),))
                SL[s]["cb_done"] = cb
                return cb

            def emit_C(s):
                h_m = {}
                for m in range(NI):
                    p = m % 2
                    P.wait("tensor", syp(p), gu_m[m])
                    if m >= 2:
                        P.wait("tensor", "sv", h_m[m - 2])
                    for k in range(NE):
                        P.do("tensor", lambda e, k=k, p=p: e.matmul(
                            gps[p][:, 0:TSL], wg_b[p][:, k * 128:(k + 1) * 128],
                            L2c(k), start=(k == 0), stop=(k == NE - 1),
                            skip_group_check=True), None)
                    for k in range(NE):
                        P.do("tensor", lambda e, k=k, p=p: e.matmul(
                            ups[p][:, 0:TSL], wu_b[p][:, k * 128:(k + 1) * 128],
                            L2c(k), start=(k == 0), stop=(k == NE - 1),
                            skip_group_check=True), None)
                    gum = P.mark("tensor", "st")
                    if m + 2 < NI:
                        P.wait("sync", "st", gum)
                        P.dma("sync", wg_b[p][:, :], wsgb[m + 2], syp(p))
                        gu_m[m + 2] = P.dma("sync", wu_b[p][:, :], wsub[m + 2], syp(p))
                    P.wait("scalar", "st", gum)
                    if m >= 2:
                        P.wait("scalar", "sv", h_m[m - 2])
                    else:
                        P.wait("scalar", "sv", om_prev[p])
                        P.wait("scalar", "sv", hr_prev[p])
                    sm = P.do("scalar", lambda e, p=p: e.activation(
                        silu_s[p][:, :], gps[p][:, 0:TSL], AF.Silu), "ss")
                    P.wait("vector", "ss", sm)
                    if H_used is not None and m == 0:
                        P.wait("vector", "st", H_used)
                        if dbg:
                            P.wait("vector", "sxd", P.cnt["sxd"])
                    h_m[m] = P.do("vector", lambda e, m=m, p=p: e.tensor_tensor(
                        Hc(m), silu_s[p][:, :], ups[p][:, 0:TSL], ALU.mult), "sv")
                SL[s]["h_m"] = h_m

            def emit_D(s):
                h_m = SL[s]["h_m"]
                rgu = {}
                P.wait("sync", "sv", h_m[NI - 2])
                P.dma("sync", wg_b[0][:, :], wrgb[0], "sya")
                rgu[0] = P.dma("sync", wu_b[0][:, :], wrub[0], "sya")
                P.wait("sync", "sv", h_m[NI - 1])
                P.dma("sync", wg_b[1][:, :], wrgb[1], "syb")
                rgu[1] = P.dma("sync", wu_b[1][:, :], wrub[1], "syb")
                hr_m = {}
                gum = None
                for m in range(NRI):
                    p = m % 2
                    P.wait("tensor", syp(p), rgu[m])
                    if m >= 2:
                        P.wait("tensor", "sv", hr_m[m - 2])
                    for k in range(NE):
                        P.do("tensor", lambda e, k=k, p=p: e.matmul(
                            gps[p][:, 0:TSL], wg_b[p][:, k * 128:(k + 1) * 128],
                            L2c(k), start=(k == 0), stop=(k == NE - 1),
                            skip_group_check=True), None)
                    for k in range(NE):
                        P.do("tensor", lambda e, k=k, p=p: e.matmul(
                            ups[p][:, 0:TSL], wu_b[p][:, k * 128:(k + 1) * 128],
                            L2c(k), start=(k == 0), stop=(k == NE - 1),
                            skip_group_check=True), None)
                    gum = P.mark("tensor", "st")
                    if m + 2 < NRI:
                        P.wait("sync", "st", gum)
                        P.dma("sync", wg_b[p][:, :], wrgb[m + 2], syp(p))
                        rgu[m + 2] = P.dma("sync", wu_b[p][:, :], wrub[m + 2], syp(p))
                    elif s + 1 < NSL and m + 2 < NRI + 2:
                        P.wait("sync", "st", gum)
                        P.dma("sync", wg_b[p][:, :], wsgb[m + 2 - NRI], syp(p))
                        gu_m[m + 2 - NRI] = P.dma("sync", wu_b[p][:, :],
                                                  wsub[m + 2 - NRI], syp(p))
                    P.wait("scalar", "st", gum)
                    if m >= 2:
                        P.wait("scalar", "sv", hr_m[m - 2])
                    else:
                        P.wait("scalar", "sv", h_m[NI - 2 + m])
                    sm = P.do("scalar", lambda e, p=p: e.activation(
                        silu_s[p][:, :], gps[p][:, 0:TSL], AF.Silu), "ss")
                    P.wait("vector", "ss", sm)
                    hr_m[m] = P.do("vector", lambda e, m=m, p=p: e.tensor_tensor(
                        Hrc(m), silu_s[p][:, :], ups[p][:, 0:TSL], ALU.mult), "sv")
                SL[s]["hr_m"] = hr_m
                hr_prev[0] = hr_m[NRI - 2]
                hr_prev[1] = hr_m[NRI - 1]
                return gum

            def emit_E(s):
                h_m = SL[s]["h_m"]
                hr_m = SL[s]["hr_m"]
                cb_done = SL[s]["cb_done"]
                dd = {}
                P.wait("gpsimd", "st", down_used[0])
                P.dma("gpsimd", wd_b[0][:, :], wsdb[0], "sga")
                dd[0] = P.dma("gpsimd", wrd_b[0][:, :], wrdb[0], "sga")
                P.wait("gpsimd", "st", down_used[1])
                P.dma("gpsimd", wd_b[1][:, :], wsdb[1], "sgb")
                dd[1] = P.dma("gpsimd", wrd_b[1][:, :], wrdb[1], "sgb")
                h_all = h_m[NI - 1]
                hr_all = hr_m[NRI - 1]
                comb_v = {}
                stat4 = {}
                sq4 = {}
                dmm = {}
                for j in range(NE):
                    p = j % 2
                    P.wait("tensor", pch(p), dd[j])
                    P.wait("tensor", "sv", h_all)
                    P.wait("tensor", "sv", hr_all)
                    if j >= 2:
                        P.wait("tensor", "sv", comb_v[j - 2])
                    for m in range(NI):
                        P.do("tensor", lambda e, m=m, p=p: e.matmul(
                            gps[p][:, 0:TSL], wd_b[p][:, m * 128:(m + 1) * 128],
                            Hc(m), start=(m == 0), stop=(m == NI - 1),
                            skip_group_check=True), None)
                    for m in range(NRI):
                        P.do("tensor", lambda e, m=m, p=p: e.matmul(
                            ups[p][:, 0:TSL], wrd_b[p][:, m * 128:(m + 1) * 128],
                            Hrc(m), start=(m == 0), stop=(m == NRI - 1),
                            skip_group_check=True), None)
                    dmm[j] = P.mark("tensor", "st")
                    if j + 2 < NE:
                        P.wait("gpsimd", "st", dmm[j])
                        P.dma("gpsimd", wd_b[p][:, :], wsdb[j + 2], pch(p))
                        dd[j + 2] = P.dma("gpsimd", wrd_b[p][:, :], wrdb[j + 2], pch(p))
                    P.wait("vector", "st", dmm[j])
                    if j == 0:
                        P.wait("vector", "sv", cb_done)
                    P.do("vector", lambda e, j=j, p=p: e.tensor_tensor(
                        Fc(j), ups[p][:, 0:TSL], bc_cmb[:, :], ALU.mult), None)
                    comb_v[j] = P.do("vector", lambda e, j=j, p=p: e.tensor_tensor(
                        Fc(j), Fc(j), gps[p][:, 0:TSL], ALU.add), "sv")
                    if j >= 2:
                        P.wait("vector", "st", stat4[j - 2])
                    sq4[j] = P.do("vector", lambda e, j=j, p=p: e.tensor_tensor(
                        fsq[p][:, :], Fc(j), Fc(j), ALU.mult), "sv")
                    P.wait("tensor", "sv", sq4[j])
                    P.do("tensor", lambda e, j=j: e.matmul(
                        st_s[0:1, 0:TSL], ones_b[:, :], Fc(j),
                        start=(j == 0), stop=(j == NE - 1),
                        skip_group_check=True), None)
                    P.do("tensor", lambda e, j=j, p=p: e.matmul(
                        st_q[0:1, 0:TSL], ones_b[:, :], fsq[p][:, :],
                        start=(j == 0), stop=(j == NE - 1),
                        skip_group_check=True), None)
                    stat4[j] = P.mark("tensor", "st")
                down_used[0] = dmm[NE - 2]
                down_used[1] = dmm[NE - 1]
                SL[s]["stat4"] = stat4
                SL[s]["comb_v"] = comb_v
                return stat4[NE - 1]

            def emit_F(s):
                stat4 = SL[s]["stat4"]
                P.wait("vector", "st", stat4[NE - 1])
                rd4, nd4 = ln_statmath(st_s[0:1, 0:TSL], st_q[0:1, 0:TSL],
                                       r4a, r4b, scr4, TSL)
                c4a = bcast(r4b, bc4r, misc, TSL,
                            pre_t=(("ss", rd4), ("sv", misc_free)))
                c4b = bcast(r4a, bc4n, misc, TSL,
                            pre_t=(("sv", nd4), ("sv", c4a)))
                SL[s]["c4"] = (c4a, c4b)
                return c4b

            def emit_G(s):
                c0 = s * TSL
                c4a, c4b = SL[s]["c4"]
                hr_m = SL[s]["hr_m"]
                xl2 = {}
                for k in range(2):
                    P.wait("sync", "sv", hr_m[NRI - 2 + k])
                    xl2[k] = P.dma("sync", silu_s[k][:, :],
                                   xres[k * 128:(k + 1) * 128, c0:c0 + TSL],
                                   "sgx1" if k == 0 else "sgx2")
                P.wait("vector", "sv", c4b)
                P.wait("vector", "sv", c4a)
                om = {}
                for j in range(NE):
                    p = j % 2
                    ch = "sgx1" if p == 0 else "sgx2"
                    och = "sxo0" if p == 0 else "sxo1"
                    P.wait("vector", ch, xl2[j])
                    P.wait("vector", och, out_free[p])
                    P.do("vector", lambda e, j=j, p=p: e.tensor_tensor(
                        out_b[p][:, :], Fc(j), bc4r[:, :], ALU.mult), None)
                    P.do("vector", lambda e, p=p: e.tensor_tensor(
                        out_b[p][:, :], out_b[p][:, :], bc4n[:, :], ALU.add), None)
                    om[j] = P.do("vector", lambda e, p=p: e.tensor_tensor(
                        out_b[p][:, :], out_b[p][:, :], silu_s[p][:, :], ALU.add), "sv")
                    if j + 2 < NE:
                        P.wait("sync", "sv", om[j])
                        xl2[j + 2] = P.dma("sync", silu_s[p][:, :],
                                           xres[(j + 2) * 128:(j + 3) * 128, c0:c0 + TSL],
                                           ch)
                    P.wait("sync", "sv", om[j])
                    out_free[p] = P.dma(
                        "sync", outT[j * 128:(j + 1) * 128, c0:c0 + TSL],
                        out_b[p][:, :], och)
                om_prev[0] = om[NE - 2]
                om_prev[1] = om[NE - 1]

            emit_A(0)
            misc_free = emit_B(0)
            if dbg:
                P.wait("sync", "sv", misc_free)
                P.dma("sync", dbg_t["d_ln2"], ln2_sl[:, :], "sxd")
                P.dma("sync", dbg_t["d_comb"], comb_sb[:, :], "sxd")
            emit_C(0)
            ln2_used = emit_D(0)
            for s in range(NSL):
                H_used = emit_E(s)
                if s + 1 < NSL:
                    emit_A(s + 1)
                    misc_free = emit_B(s + 1)
                misc_free = emit_F(s)
                emit_G(s)
                if dbg and s == 0:
                    P.wait("sync", "sv", SL[0]["comb_v"][NE - 1])
                    P.dma("sync", dbg_t["d_h2"], B[:, 0:2 * TSL], "sxd")
                    P.dma("sync", dbg_t["d_ffn"], ffn_sl[:, :], "sxd")
                if s + 1 < NSL:
                    emit_C(s + 1)
                    ln2_used = emit_D(s + 1)
            P.wait("sync", "sxo0", P.cnt["sxo0"])
            P.wait("sync", "sxo1", P.cnt["sxo1"])
            P.wait("sync", "sxw", P.cnt["sxw"])
            if dbg:
                P.wait("sync", "sxd", P.cnt["sxd"])
            P.flush("ffn")

    return nc



def _prep_km(W):
    K, M = W.shape
    return np.ascontiguousarray(
        W.reshape(K // 128, 128, M // 128, 128).transpose(2, 1, 0, 3)
         .reshape(M // 128, 128, K))


def _common_inputs(W_qkv, b_qkv, W_out, b_out, router_weight,
                   sh_gate, sh_up, sh_down, r_gate, r_up, r_down):
    f32 = np.float32
    return {
        "wqkv": _prep_km(np.asarray(W_qkv, f32)),
        "bqkv": np.ascontiguousarray(np.asarray(b_qkv, f32).reshape(48, 128).T),
        "wout": _prep_km(np.asarray(W_out, f32)),
        "bout": np.ascontiguousarray(np.asarray(b_out, f32).reshape(16, 128).T),
        "routr": np.ascontiguousarray(
            np.asarray(router_weight, f32).T.reshape(16, 128, 8)
            .transpose(1, 0, 2).reshape(128, 128)),
        "wsg": _prep_km(np.ascontiguousarray(np.asarray(sh_gate, f32).T)),
        "wsu": _prep_km(np.ascontiguousarray(np.asarray(sh_up, f32).T)),
        "wsd": _prep_km(np.ascontiguousarray(np.asarray(sh_down, f32).T)),
        "wrg": _prep_km(np.ascontiguousarray(np.asarray(r_gate, f32).T)),
        "wru": _prep_km(np.ascontiguousarray(np.asarray(r_up, f32).T)),
        "wrd": _prep_km(np.ascontiguousarray(np.asarray(r_down, f32).T)),
        "identb": np.eye(128, dtype=ml_dtypes.bfloat16),
        "identf": np.eye(128, dtype=f32),
        "causal": np.where(np.tril(np.ones((128, 128), dtype=bool)),
                           np.float32(0.0), np.float32(NEG)).astype(f32),
    }


def kernel(L0_attn_inX, L0_attn_W_qkv, L0_attn_b_qkv, L0_attn_W_out,
           L0_attn_b_out, L0_ffn_router_weight, L0_ffn_sh_gate, L0_ffn_sh_up,
           L0_ffn_sh_down, L0_ffn_r_gate, L0_ffn_r_up, L0_ffn_r_down):
    S = 2048
    ncores = 8
    x = np.asarray(L0_attn_inX, np.float32)
    cm = _common_inputs(L0_attn_W_qkv, L0_attn_b_qkv, L0_attn_W_out,
                        L0_attn_b_out, L0_ffn_router_weight, L0_ffn_sh_gate,
                        L0_ffn_sh_up, L0_ffn_sh_down, L0_ffn_r_gate,
                        L0_ffn_r_up, L0_ffn_r_down)
    in_maps = []
    for c in range(ncores):
        m = dict(cm)
        m["xT"] = np.ascontiguousarray(x[c].T)
        in_maps.append(m)
    nc = build_program(S)
    res = run_bass_kernel_spmd(nc, in_maps, core_ids=list(range(ncores)))
    out = np.stack([np.asarray(r["outT"]).T for r in res.results], axis=0)
    return np.ascontiguousarray(out.astype(np.float32))
